# version 17
# baseline (speedup 1.0000x reference)
import numpy as np
import ml_dtypes
from contextlib import ExitStack

import concourse.bass as bass
import concourse.mybir as mybir
from concourse.bass_utils import run_bass_kernel_spmd

F32 = mybir.dt.float32
BF16 = mybir.dt.bfloat16
AF = mybir.ActivationFunctionType
ALU = mybir.AluOpType
AX = mybir.AxisListType
bfdt = ml_dtypes.bfloat16

NCORES = 8
B, S, DM = 2, 4096, 2048
H, DH, G, RANK = 16, 128, 4, 512
NDC = DM // 128
TOK = 1024
NTC = TOK // 128
NQT = TOK // 512
NKC = S // 128
SCALE = float(DH) ** -0.5
HF = 8192
NHC = HF // 128
EPS_LN = 1e-5
EPS_RMS = float(np.finfo(np.float32).eps)
SIG1 = float(1.0 / (1.0 + np.exp(-1.0)))
GSH = 2048
OSH = 256

ENGS = ("tensor", "vector", "scalar", "gpsimd", "sync")


class _StopBuild(Exception):
    pass

DEBUG_DUMPS = []
STOP_PHASE = None


CHAIN_SEM = {"vector": "pv", "scalar": "pa", "gpsimd": "pg"}


class Plan:
    def __init__(self, nc, ctx):
        self.nc = nc
        self.ctx = ctx
        self.ops = []
        self.cnt = {}
        self.h = {}
        self.chain = {"vector": None, "scalar": None, "gpsimd": None}

    def sem(self, name):
        self.h[name] = self.ctx.enter_context(self.nc.semaphore(name))
        self.cnt[name] = 0

    def op(self, eng, fn, wait=(), inc=None, n=1, is_dma=False):
        wait = [w for w in wait if w is not None]
        if eng in CHAIN_SEM and not is_dma:
            if inc is None:
                inc = CHAIN_SEM[eng]
            if self.chain[eng] is not None:
                wait.append(self.chain[eng])
        tok = None
        if inc is not None:
            self.cnt[inc] += n
            tok = (inc, self.cnt[inc])
        if eng in CHAIN_SEM and not is_dma:
            self.chain[eng] = tok
        self.ops.append((eng, fn, tuple(wait), inc, n))
        return tok

    def dma(self, eng, out, in_, wait=(), inc=None):
        return self.op(eng, (lambda e, o=out, i=in_: e.dma_start(out=o, in_=i)),
                       wait, inc, 16, is_dma=True)

    def barrier(self):
        waits = tuple((s, c) for s, c in self.cnt.items()
                      if c > 0 and not s.startswith("cc"))
        for e in ENGS:
            self.op(e, lambda eng: eng.nop(), wait=waits)

    def mark(self, name):
        if STOP_PHASE == name and not hasattr(self, "trunc"):
            self.trunc = (len(self.ops), dict(self.cnt))

    def finalize(self):
        if hasattr(self, "trunc"):
            idx, cnt = self.trunc
            self.ops = self.ops[:idx]
            waits = tuple((s, c) for s, c in cnt.items() if c > 0)
            for e in ENGS:
                self.ops.append((e, lambda eng: eng.nop(), waits, None, 1))

    def run(self, engobj, name):
        seen = {}
        for (e, fn, waits, inc, n) in self.ops:
            if e != name:
                continue
            for (s, v) in waits:
                if seen.get(s, -1) < v:
                    engobj.wait_ge(self.h[s], v)
                    seen[s] = v
            ins = fn(engobj)
            if inc is not None:
                ins.then_inc(self.h[inc], n)


def build_program():
    nc = bass.Bass("TRN2", target_bir_lowering=False)

    x_own_d = nc.dram_tensor("x_own", [TOK, DM], F32, kind="ExternalInput").ap()
    Wq_d = nc.dram_tensor("Wq", [DM, DM], F32, kind="ExternalInput").ap()
    Wc_d = nc.dram_tensor("Wc", [DM, RANK], F32, kind="ExternalInput").ap()
    Wup_d = nc.dram_tensor("Wup", [RANK, RANK], F32, kind="ExternalInput").ap()
    Wo_d = nc.dram_tensor("Wo", [DM, DM], F32, kind="ExternalInput").ap()
    wgu_d = nc.dram_tensor("wgu_sh", [GSH, DM], F32, kind="ExternalInput").ap()
    wp_d = nc.dram_tensor("wp_sh", [OSH, HF], F32, kind="ExternalInput").ap()
    ones_b_d = nc.dram_tensor("ones_b", [128, 128], BF16, kind="ExternalInput").ap()
    id_b_d = nc.dram_tensor("id_b", [128, 128], BF16, kind="ExternalInput").ap()
    id_f_d = nc.dram_tensor("id_f", [128, 128], F32, kind="ExternalInput").ap()
    ones_f_d = nc.dram_tensor("ones_f", [1, 128], F32, kind="ExternalInput").ap()
    mask_d = nc.dram_tensor("mask", [128, 2], F32, kind="ExternalInput").ap()
    out_d = nc.dram_tensor("out", [TOK, DM], F32, kind="ExternalOutput").ap()

    wguT_loc = nc.dram_tensor("wguT_loc", [DM, GSH], BF16)
    wpT_loc = nc.dram_tensor("wpT_loc", [HF, OSH], BF16)
    wguT_all = nc.dram_tensor("wguT_all", [NCORES * DM, GSH], BF16)
    wpT_all = nc.dram_tensor("wpT_all", [NCORES * HF, OSH], BF16)
    kv_gin = nc.dram_tensor("kv_gin", [128, 2 * G * TOK + 256], BF16)
    kv_gall = nc.dram_tensor("kv_gall", [NCORES * 128, 2 * G * TOK + 256], BF16)

    with ExitStack() as ctx:
        p = Plan(nc, ctx)
        for s in ("dq", "aq", "gq", "sq", "pt", "pv", "pa", "pg",
                  "cc1", "cc2", "cc3", "wc", "g0", "g1", "w0", "w1", "w2",
                  "o0", "o1", "q0", "q1", "q2", "a0", "a1", "a2", "a3",
                  "t0", "t1", "s0"):
            p.sem(s)

        lo = (nc._sbuf_addr_for_side("left") + 31) & ~31
        hi = nc._sbuf_addr_for_side("right")
        arena_free = [(lo, hi)]
        sb_map = {}
        ps_live = {}
        uid = [0]

        def _arena_alloc(nbytes, side="left"):
            nbytes = (nbytes + 31) & ~31
            order = range(len(arena_free)) if side == "left" else \
                range(len(arena_free) - 1, -1, -1)
            for idx in order:
                a, b = arena_free[idx]
                if b - a >= nbytes:
                    if side == "left":
                        addr = a
                        arena_free[idx] = (a + nbytes, b)
                    else:
                        addr = b - nbytes
                        arena_free[idx] = (a, b - nbytes)
                    if arena_free[idx][0] == arena_free[idx][1]:
                        arena_free.pop(idx)
                    return addr, nbytes
            raise MemoryError(f"SBUF arena full ({nbytes}B)")

        def _arena_release(a, n):
            arena_free.append((a, a + n))
            arena_free.sort()
            i = 0
            while i + 1 < len(arena_free):
                x, y = arena_free[i], arena_free[i + 1]
                if x[1] == y[0]:
                    arena_free[i] = (x[0], y[1])
                    arena_free.pop(i + 1)
                else:
                    i += 1

        def salloc(name, shape, dtype, side="left"):
            per = int(np.prod(shape[1:])) * (4 if dtype == F32 else 2)
            try:
                addr, n = _arena_alloc(per, side)
            except MemoryError:
                used = sorted(sb_map.items(), key=lambda kv: kv[1][0])
                raise MemoryError(
                    f"SBUF full allocating {name} ({per}B); live: "
                    + ", ".join(f"{k}@{a}+{s}" for k, (a, s) in used))
            sb_map[name] = (addr, n)
            uid[0] += 1
            return nc.alloc_sbuf_tensor_at(f"{name}_u{uid[0]}", shape, dtype,
                                           offset=addr)

        def palloc(name, shape, dtype=F32):
            uid[0] += 1
            g = nc.psum_tensor(f"{name}_u{uid[0]}", shape, dtype)
            t = g.__enter__()
            ps_live[name] = (g, uid[0])
            return t

        def sfree(*names):
            ps = [n for n in names if n in ps_live]
            for n in sorted(ps, key=lambda n: -ps_live[n][1]):
                ps_live.pop(n)[0].__exit__(None, None, None)
            for n in names:
                if n in sb_map:
                    a, sz = sb_map.pop(n)
                    _arena_release(a, sz)

        def dbg(tag, t, shape, dtype):
            if tag not in DEBUG_DUMPS:
                return
            d = nc.dram_tensor(f"dbg_{tag}", shape, dtype,
                               kind="ExternalOutput").ap()
            p.dma("scalar", d, t[:, 0:shape[1]], inc="sq")

        ones_b = salloc("ones_b_s", [128, 128], BF16)
        id_b = salloc("id_b_s", [128, 128], BF16)
        id_f = salloc("id_f_s", [128, 128], F32)
        ones_f = salloc("ones_f_s", [1, 128], F32)
        msk = salloc("msk_s", [128, 2], F32)
        sm = salloc("sm", [128, 128], F32, side="right")
        rowbuf = salloc("rowbuf", [1, 1024], F32, side="right")

        p.dma("sync", ones_b[:], ones_b_d, inc="dq")
        p.dma("sync", id_b[:], id_b_d, inc="dq")
        p.dma("sync", id_f[:], id_f_d, inc="dq")
        p.dma("sync", msk[:], mask_d, inc="dq")
        t_const = p.dma("sync", ones_f[:], ones_f_d, inc="dq")
        for e in ("tensor", "vector", "scalar", "gpsimd"):
            p.op(e, lambda eng: eng.nop(), wait=(t_const,))

        wt = [salloc(f"q_wt{i}", [128, HF], BF16) for i in range(2)]
        xq = salloc("q_xq", [128, HF], BF16)
        abuf = salloc("q_a", [128, HF], BF16)
        bbuf = salloc("q_b", [128, HF], BF16)
        slabT = salloc("q_slab", [128, HF], BF16)
        gsum = salloc("q_gsum", [128, 128], F32)
        gscl = salloc("q_gscl", [128, 128], BF16)
        grs = salloc("q_grs", [128, 128], F32)
        ps_t = [palloc(f"q_pst{i}", [128, 128], BF16) for i in range(2)]

        qstate = {"i": 0, "wt_war": [None, None], "xq_war": None, "st": None,
                  "ps_war": [None, None]}

        def quant_tile(src_rows_ap, wide, ndc, dst_col_ap):
            i = qstate["i"]
            qstate["i"] += 1
            buf = wt[i % 2]
            ng = wide // 64
            t_ld = p.dma("gpsimd", buf[:, 0:wide], src_rows_ap,
                         wait=(qstate["wt_war"][i % 2],), inc=f"t{i % 2}")
            b3 = buf[:, 0:wide].rearrange("p (g k) -> p g k", k=64)
            t1 = p.op("vector", lambda e, b3=b3, ng=ng: e.reduce_sum(
                gsum[:, 0:ng], b3, axis=AX.X, apply_absolute_value=True),
                wait=(t_ld,), inc="pv")
            t2 = p.op("vector", lambda e, ng=ng: e.tensor_scalar(
                out=gscl[:, 0:ng], in0=gsum[:, 0:ng],
                scalar1=1.0 / 64.0, scalar2=1e-8, op0=ALU.mult, op1=ALU.max),
                wait=(t1,), inc="pv")
            t3 = p.op("vector", lambda e, ng=ng: e.reciprocal(
                grs[:, 0:ng], gscl[:, 0:ng]), wait=(t2,), inc="pv")
            rs3 = grs[:, 0:ng].unsqueeze(2).broadcast_to([128, ng, 64])
            x3 = xq[:, 0:wide].rearrange("p (g k) -> p g k", k=64)
            t4 = p.op("vector", lambda e, b3=b3, rs3=rs3, x3=x3: e.tensor_tensor(
                x3, b3, rs3, ALU.mult), wait=(t3, qstate["xq_war"]), inc="pv")
            qstate["wt_war"][i % 2] = t4
            p.op("vector", lambda e, wide=wide: e.tensor_single_scalar(
                abuf[:, 0:wide], xq[:, 0:wide], 0.5, ALU.is_gt), inc="pv")
            p.op("vector", lambda e, wide=wide: e.tensor_single_scalar(
                bbuf[:, 0:wide], xq[:, 0:wide], -0.5, ALU.is_lt), inc="pv")
            p.op("vector", lambda e, wide=wide: e.tensor_tensor(
                abuf[:, 0:wide], abuf[:, 0:wide], bbuf[:, 0:wide], ALU.subtract),
                inc="pv")
            sc3 = gscl[:, 0:ng].unsqueeze(2).broadcast_to([128, ng, 64])
            a3 = abuf[:, 0:wide].rearrange("p (g k) -> p g k", k=64)
            t8 = p.op("vector", lambda e, x3=x3, a3=a3, sc3=sc3: e.tensor_tensor(
                x3, a3, sc3, ALU.mult), inc="pv")
            ev = None
            ttr = None
            evs = []
            for dc in range(ndc):
                wtr = [t8]
                if dc >= 2:
                    wtr.append(evs[dc - 2])
                else:
                    wtr.append(qstate["ps_war"][dc % 2])
                ttr = p.op("tensor", lambda e, dc=dc: e.transpose(
                    ps_t[dc % 2][:], xq[:, dc * 128:(dc + 1) * 128], id_b[:]),
                    wait=tuple(wtr), inc="pt")
                ev = p.op("scalar", lambda e, dc=dc: e.activation(
                    slabT[:, dc * 128:(dc + 1) * 128], ps_t[dc % 2][:], AF.Copy),
                    wait=(ttr, qstate["st"]), inc="pa")
                evs.append(ev)
            qstate["xq_war"] = ttr
            qstate["ps_war"] = [evs[-2], evs[-1]]
            st = None
            grp = ndc // 4
            for piece in range(4):
                st = p.dma(
                    "sync",
                    dst_col_ap[:, piece * grp:(piece + 1) * grp, :],
                    slabT[:, piece * grp * 128:(piece + 1) * grp * 128]
                    .rearrange("p (dc hh) -> p dc hh", hh=128),
                    wait=(ev,) if piece == 0 else (), inc="s0")
            qstate["st"] = st
            return st

        for i in range(16):
            dst = wguT_loc.ap()[:, i * 128:(i + 1) * 128].rearrange(
                "(dc pp) hh -> pp dc hh", pp=128)
            last_store = quant_tile(wgu_d[i * 128:(i + 1) * 128, :], DM, NDC, dst)
        for i in range(2):
            dst = wpT_loc.ap()[:, i * 128:(i + 1) * 128].rearrange(
                "(dc pp) hh -> pp dc hh", pp=128)
            last_store = quant_tile(wp_d[i * 128:(i + 1) * 128, :], HF, NHC, dst)

        Wc_bf = salloc("Wc_bf", [128, NDC * RANK], BF16)
        Wup_bf = salloc("Wup_bf", [128, 4 * RANK], BF16)
        for dc in range(NDC):
            p.dma("gpsimd", Wc_bf[:, dc * RANK:(dc + 1) * RANK],
                  Wc_d[dc * 128:(dc + 1) * 128, :], inc="wc")
        for rc in range(4):
            p.dma("gpsimd", Wup_bf[:, rc * RANK:(rc + 1) * RANK],
                  Wup_d[rc * 128:(rc + 1) * 128, :], inc="wc")
        t_wck = ("wc", p.cnt["wc"])

        p.barrier()
        p.mark("Q")
        sfree("q_wt0", "q_wt1", "q_xq", "q_a", "q_b", "q_slab",
              "q_gsum", "q_gscl", "q_grs", "q_pst0", "q_pst1")

        sq_scr = salloc("sq_scr", [128, DM], F32)
        ln_war = {"tok": None}

        def ln_stats_chunk(xc_ap, col, tok_ld, scr):
            t1 = p.op("vector", lambda e, xc_ap=xc_ap, col=col: e.reduce_sum(
                sm[:, col:col + 1], xc_ap, axis=AX.X),
                wait=(tok_ld,), inc="pv")
            t2 = p.op("scalar", lambda e, xc_ap=xc_ap, col=col, scr=scr:
                      e.activation(
                scr[:], xc_ap, AF.Square, accum_out=sm[:, 32 + col:33 + col]),
                wait=(tok_ld, ln_war["tok"]), inc="pa")
            return t1, t2

        def ln_finish(nch, toks):
            w = tuple(toks)
            c0, c1, c2, c3 = 0, 32, 64, 96
            p.op("vector", lambda e: e.tensor_scalar(
                out=sm[:, c0:c0 + nch], in0=sm[:, c0:c0 + nch],
                scalar1=1.0 / DM, scalar2=None, op0=ALU.mult), wait=w, inc="pv")
            tsn = p.op("vector", lambda e: e.tensor_scalar(
                out=sm[:, c1:c1 + nch], in0=sm[:, c1:c1 + nch],
                scalar1=1.0 / DM, scalar2=None, op0=ALU.mult), inc="pv")
            ln_war["tok"] = tsn
            p.op("vector", lambda e: e.tensor_tensor(
                sm[:, c3:c3 + nch], sm[:, c0:c0 + nch], sm[:, c0:c0 + nch],
                ALU.mult), inc="pv")
            p.op("vector", lambda e: e.tensor_tensor(
                sm[:, c1:c1 + nch], sm[:, c1:c1 + nch], sm[:, c3:c3 + nch],
                ALU.subtract), inc="pv")
            tv = p.op("vector", lambda e: e.tensor_scalar(
                out=sm[:, c1:c1 + nch], in0=sm[:, c1:c1 + nch],
                scalar1=EPS_LN, scalar2=None, op0=ALU.add), inc="pv")
            ts_ = p.op("scalar", lambda e: e.activation(
                sm[:, c3:c3 + nch], sm[:, c1:c1 + nch], AF.Sqrt),
                wait=(tv,), inc="pa")
            tr = p.op("vector", lambda e: e.reciprocal(
                sm[:, c2:c2 + nch], sm[:, c3:c3 + nch]), wait=(ts_,), inc="pv")
            p.op("vector", lambda e: e.tensor_tensor(
                sm[:, c3:c3 + nch], sm[:, c2:c2 + nch], sm[:, c2:c2 + nch],
                ALU.mult), wait=(tr,), inc="pv")
            p.op("vector", lambda e: e.tensor_tensor(
                sm[:, c3:c3 + nch], sm[:, c3:c3 + nch], sm[:, c1:c1 + nch],
                ALU.mult), inc="pv")
            p.op("vector", lambda e: e.tensor_scalar(
                out=sm[:, c3:c3 + nch], in0=sm[:, c3:c3 + nch],
                scalar1=-0.5, scalar2=1.5, op0=ALU.mult, op1=ALU.add), inc="pv")
            return p.op("vector", lambda e: e.tensor_tensor(
                sm[:, c2:c2 + nch], sm[:, c3:c3 + nch], sm[:, c2:c2 + nch],
                ALU.mult), inc="pv")

        def ln_transpose_chunk(nb, dst, dst_col_stride, tc, ta, war_ring,
                               dst2=None, tc2=0):
            evs = []
            for dc in range(NDC):
                wtr = [ta]
                if dc >= 2:
                    wtr.append(evs[dc - 2])
                elif war_ring[dc % 2] is not None:
                    wtr.append(war_ring[dc % 2])
                ttr = p.op("tensor", lambda e, nb=nb, dc=dc: e.transpose(
                    ps_tr[dc % 2][:], nb[:, dc * 128:(dc + 1) * 128], id_b[:]),
                    wait=tuple(wtr), inc="pt")
                d = dst[:, dc * dst_col_stride + tc * 128:
                        dc * dst_col_stride + tc * 128 + 128]
                if dc % 2 == 0:
                    ev = p.op("vector", lambda e, dc=dc, d=d: e.tensor_copy(
                        d, ps_tr[dc % 2][:]), wait=(ttr,), inc="pv")
                else:
                    ev = p.op("scalar", lambda e, dc=dc, d=d: e.activation(
                        d, ps_tr[dc % 2][:], AF.Copy), wait=(ttr,), inc="pa")
                evs.append(ev)
            war_ring[0] = evs[-2]
            war_ring[1] = evs[-1]
            return evs[-1]

        xn1T_own = salloc("xn1T_own", [128, NDC * TOK], BF16)
        K_loc = salloc("K_loc", [128, G * TOK], BF16)
        VT_loc = salloc("VT_loc", [128, 8 * RANK], BF16)
        xc = [salloc(f"a_xc{i}", [128, DM], F32) for i in range(4)]
        xnb = [salloc(f"a_xnb{i}", [128, DM], BF16) for i in range(2)]
        kvc = salloc("a_kvc", [128, 4 * 512], BF16)
        kmax2 = salloc("kmax2", [1, 8], F32, side="right")
        klb = salloc("klb", [1, 4], BF16, side="right")
        ps_tr = [palloc(f"ps_tr{i}", [128, 128], BF16) for i in range(2)]
        ps_mm = [palloc(f"ps_mm{i}", [128, 512], F32) for i in range(2)]

        xc_war = [None] * 4
        xnb_war = [None, None]
        tr_war = [None, None]
        mm_war = [None, None]
        prev_ta = None
        last_k_ev = None
        for j in range(2):
            ch = []
            for c in range(4):
                cidx = j * 4 + c
                buf = xc[c]
                t_ld = p.dma("scalar", buf[:],
                             x_own_d[cidx * 128:(cidx + 1) * 128, :],
                             wait=(xc_war[c],), inc=f"a{c}")
                t1, t2 = ln_stats_chunk(buf[:], c, t_ld, sq_scr)
                ch.append((buf, c, t1, t2))
            t_stats = ln_finish(4, [t for (_, _, t1, t2) in ch
                                    for t in (t1, t2)] + [prev_ta])
            last_ev = None
            for (buf, c, _, _) in ch:
                nb = xnb[c % 2]
                wpe = [t_stats, xnb_war[c % 2]]
                ta = p.op("vector", lambda e, buf=buf, nb=nb, c=c: e.tensor_scalar(
                    out=nb[:], in0=buf[:],
                    scalar1=sm[:, c:c + 1], scalar2=sm[:, 64 + c:65 + c],
                    op0=ALU.subtract, op1=ALU.mult), wait=tuple(wpe), inc="pv")
                xc_war[c] = ta
                last_ev = ln_transpose_chunk(
                    nb, xn1T_own, TOK, j * 4 + c, ta, tr_war)
                xnb_war[c % 2] = last_ev
                prev_ta = ta
            if j == 1:
                t_xn1own = last_ev
            kv_evs = []
            for rc in range(4):
                mm = None
                for dc in range(NDC):
                    w = (last_ev, t_wck, mm_war[rc % 2]) if dc == 0 else ()
                    mm = p.op("tensor", lambda e, rc=rc, dc=dc, j=j: e.matmul(
                        ps_mm[rc % 2][:],
                        Wc_bf[:, dc * RANK + rc * 128: dc * RANK + rc * 128 + 128],
                        xn1T_own[:, dc * TOK + j * 512: dc * TOK + (j + 1) * 512],
                        start=(dc == 0), stop=(dc == NDC - 1)),
                        wait=w, inc=("pt" if dc == NDC - 1 else None))
                ev = p.op("vector", lambda e, rc=rc: e.tensor_scalar(
                    out=kvc[:, rc * 512:(rc + 1) * 512], in0=ps_mm[rc % 2][:],
                    scalar1=SIG1, scalar2=None, op0=ALU.mult),
                    wait=(mm,), inc="pv")
                kv_evs.append(ev)
                mm_war[rc % 2] = ev
            t_kvc = kv_evs[-1]
            for g in range(G):
                mm = None
                for rc in range(4):
                    w = (t_kvc, mm_war[g % 2]) if rc == 0 else ()
                    mm = p.op("tensor", lambda e, g=g, rc=rc: e.matmul(
                        ps_mm[g % 2][:],
                        Wup_bf[:, rc * RANK + g * 128: rc * RANK + g * 128 + 128],
                        kvc[:, rc * 512:(rc + 1) * 512],
                        start=(rc == 0), stop=(rc == 3)),
                        wait=w, inc=("pt" if rc == 3 else None))
                ev = p.op("vector", lambda e, g=g, j=j: e.tensor_copy(
                    K_loc[:, g * TOK + j * 512: g * TOK + (j + 1) * 512],
                    ps_mm[g % 2][:]), wait=(mm,), inc="pv")
                mm_war[g % 2] = ev
                last_k_ev = ev
            for tc in range(4):
                mm = None
                for rc in range(4):
                    w = (t_kvc, mm_war[tc % 2]) if rc == 0 else ()
                    mm = p.op("tensor", lambda e, tc=tc, rc=rc: e.matmul(
                        ps_mm[tc % 2][:],
                        kvc[:, rc * 512 + tc * 128: rc * 512 + tc * 128 + 128],
                        Wup_bf[:, rc * RANK:(rc + 1) * RANK],
                        start=(rc == 0), stop=(rc == 3)),
                        wait=w, inc=("pt" if rc == 3 else None))
                ev = p.op("vector", lambda e, tc=tc, j=j: e.tensor_copy(
                    VT_loc[:, (j * 4 + tc) * RANK:(j * 4 + tc + 1) * RANK],
                    ps_mm[tc % 2][:]), wait=(mm,), inc="pv")
                mm_war[tc % 2] = ev
        t_kvdone = ev

        ksq = salloc("ksq", [128, G * TOK], BF16)
        ps_row = palloc("ps_row", [1, 512], F32)
        ev = None
        tqk = p.op("scalar", lambda e: e.activation(
            ksq[:], K_loc[:], AF.Square), wait=(last_k_ev,), inc="pa")
        for g in range(G):
            for sl in range(2):
                mm = p.op("tensor", lambda e, g=g, sl=sl: e.matmul(
                    ps_row[:], ones_b[:, 0:1],
                    ksq[:, g * TOK + sl * 512: g * TOK + (sl + 1) * 512],
                    start=True, stop=True), wait=(tqk, ev), inc="pt")
                ev = p.op("vector", lambda e, sl=sl: e.tensor_copy(
                    rowbuf[0:1, sl * 512:(sl + 1) * 512], ps_row[:]),
                    wait=(mm,), inc="pv")
            tm = p.op("vector", lambda e, g=g: e.reduce_max(
                kmax2[0:1, 4 + g:5 + g], rowbuf[0:1, 0:TOK], axis=AX.X),
                wait=(ev,), inc="pv")
        t_klb = p.op("vector", lambda e: e.tensor_copy(
            klb[0:1, :], kmax2[0:1, 4:8]), wait=(tm,), inc="pv")
        q2s_seed = (ev, mm)

        t_kst = p.dma("sync", kv_gin.ap()[:, 0:G * TOK], K_loc[:],
                      wait=(last_k_ev,), inc="dq")
        t_vst = p.dma("sync", kv_gin.ap()[:, G * TOK:2 * G * TOK], VT_loc[:],
                      wait=(t_kvdone,), inc="dq")
        t_mst = p.dma("sync", kv_gin.ap()[0:1, 2 * G * TOK:2 * G * TOK + 4],
                      klb[0:1, :], wait=(t_klb,), inc="dq")

        p.barrier()
        p.mark("A")
        sfree("a_xc0", "a_xc1", "a_xc2", "a_xc3", "a_xnb0", "a_xnb1",
              "a_kvc", "Wc_bf", "Wup_bf", "K_loc", "VT_loc", "ksq",
              "sq_scr", "klb")

        K_sb = salloc("K_sb", [128, G * S], BF16)
        VT_sb = salloc("VT_sb", [128, NKC * RANK], BF16)
        q_bf = salloc("q_bf", [128, H * TOK], BF16)
        wqs = [salloc(f"wqs{i}", [128, NDC * 512], BF16) for i in range(3)]
        qsq = salloc("qsq", [128, TOK], BF16)
        qmax2 = salloc("qmax2", [1, 16], F32, side="right")
        km16 = salloc("km16", [1, 16], BF16, side="right")
        rowbuf2 = salloc("rowbuf2", [1, 2048], F32, side="right")
        ps_q = [palloc(f"ps_q{i}", [128, 512], F32) for i in range(2)]

        slab_toks = []
        slab_last_mm = [None] * 4
        for os_ in range(3):
            t_slab = None
            for dc in range(NDC):
                t_slab = p.dma(
                    "gpsimd", wqs[os_][:, dc * 512:(dc + 1) * 512],
                    Wq_d[dc * 128:(dc + 1) * 128, os_ * 512:(os_ + 1) * 512],
                    inc=f"q{os_}")
            slab_toks.append(t_slab)
        t_cck = p.op("gpsimd", lambda e: e.collective_compute(
            "AllGather", ALU.bypass,
            replica_groups=[list(range(NCORES))],
            ins=[kv_gin.ap().opt()], outs=[kv_gall.ap().opt()]),
            wait=(t_kst, t_vst, t_mst), inc="cc3", is_dma=True)
        slab3_dma_slot = len(p.ops)
        t_cc1 = p.op("gpsimd", lambda e: e.collective_compute(
            "AllGather", ALU.bypass, replica_groups=[list(range(NCORES))],
            ins=[wguT_loc.ap().opt()], outs=[wguT_all.ap().opt()]),
            wait=(last_store,), inc="cc1", is_dma=True)
        t_cc2 = p.op("gpsimd", lambda e: e.collective_compute(
            "AllGather", ALU.bypass, replica_groups=[list(range(NCORES))],
            ins=[wpT_loc.ap().opt()], outs=[wpT_all.ap().opt()]),
            inc="cc2", is_dma=True)
        km16b = salloc("km16b", [1, 16], BF16, side="right")
        slab_done = [None] * 4
        kvv = kv_gall.ap()
        for r in range(4):
            p.dma("sync",
                  K_sb[:, :].rearrange("p (g rr t) -> p g rr t", g=G, rr=4)
                  [:, :, r, :],
                  kvv[r * 128:(r + 1) * 128, 0:G * TOK]
                  .rearrange("p (g t) -> p g t", t=TOK),
                  wait=(t_cck,) if r == 0 else (), inc="aq")
        for r in range(4):
            p.dma("sync", VT_sb[:, r * 8 * RANK:(r + 1) * 8 * RANK],
                  kvv[r * 128:(r + 1) * 128, G * TOK:2 * G * TOK], inc="aq")
        p.dma("sync",
              km16[0:1, :].rearrange("p (r c) -> p r c", c=4),
              kvv.rearrange("(r pp) c -> pp r c", pp=128)
              [0:1, 0:4, 2 * G * TOK:2 * G * TOK + 4],
              inc="aq")
        p.dma("sync",
              km16b[0:1, :].rearrange("p (r c) -> p r c", c=4),
              kvv.rearrange("(r pp) c -> pp r c", pp=128)
              [0:1, 4:8, 2 * G * TOK:2 * G * TOK + 4],
              inc="aq")
        t_Aall = ("aq", p.cnt["aq"])

        if "kvgall" in DEBUG_DUMPS:
            dkv = nc.dram_tensor("dbg_kvgall", [NCORES * 128, 2 * G * TOK],
                                 BF16, kind="ExternalOutput").ap()
            p.dma("scalar", dkv, kvv[:, 0:2 * G * TOK],
                  wait=(t_cck,), inc="sq")
        if "kvgin" in DEBUG_DUMPS:
            dki = nc.dram_tensor("dbg_kvgin", [128, 2 * G * TOK],
                                 BF16, kind="ExternalOutput").ap()
            p.dma("scalar", dki, kv_gin.ap()[:, 0:2 * G * TOK],
                  wait=(t_kst, t_vst, t_mst), inc="sq")

        def stage_slot(i):
            buf = wqs[1 + (i % 4) // 2]
            off = (i % 2) * (G * TOK)
            return buf[:, off:off + G * TOK], slab_done[1 + (i % 4) // 2]

        def emit_kv_blend():
            t_m0k = p.op("vector", lambda e: e.tensor_scalar(
                out=K_sb[:], in0=K_sb[:], scalar1=msk[:, 0:1], scalar2=None,
                op0=ALU.mult), wait=(t_Aall,), inc="pv")
            t_m0v = p.op("vector", lambda e: e.tensor_scalar(
                out=VT_sb[:], in0=VT_sb[:], scalar1=msk[:, 0:1], scalar2=None,
                op0=ALU.mult), inc="pv")
            war = [None] * 4
            wave_tok = [None, None]
            for wv in range(2):
                for i in range(wv * 4, wv * 4 + 4):
                    r = i % 4
                    slot, t_free = stage_slot(i)
                    src = kvv[(4 + r) * 128:(5 + r) * 128,
                              (0 if wv == 0 else G * TOK):
                              (G * TOK if wv == 0 else 2 * G * TOK)]
                    wave_tok[wv] = p.dma("sync", slot, src,
                                         wait=(war[i % 4], t_free), inc="dq")
                if wv == 0:
                    for i in range(4):
                        r = i % 4
                        slot, _ = stage_slot(i)
                        dst = K_sb[:, :].rearrange(
                            "p (g rr t) -> p g rr t", g=G, rr=4)[:, :, r, :]
                        sview = slot.rearrange("p (g t) -> p g t", t=TOK)
                        war[i] = p.op(
                            "vector",
                            lambda e, dst=dst, sview=sview: e.scalar_tensor_tensor(
                                out=dst, in0=sview, scalar=msk[:, 1:2], in1=dst,
                                op0=ALU.mult, op1=ALU.add),
                            wait=(wave_tok[0], t_m0k), inc="pv")
            last = None
            for i in range(4):
                slot, _ = stage_slot(i + 4)
                dst = VT_sb[:, i * 8 * RANK:(i + 1) * 8 * RANK]
                last = p.op(
                    "vector",
                    lambda e, dst=dst, slot=slot: e.scalar_tensor_tensor(
                        out=dst, in0=slot, scalar=msk[:, 1:2], in1=dst,
                        op0=ALU.mult, op1=ALU.add),
                    wait=(wave_tok[1], t_m0v), inc="pv")
            p.op("vector", lambda e: e.tensor_scalar(
                out=km16[0:1, :], in0=km16[0:1, :], scalar1=msk[0:1, 0:1],
                scalar2=None, op0=ALU.mult), inc="pv")
            return p.op("vector", lambda e: e.scalar_tensor_tensor(
                out=km16[0:1, :], in0=km16b[0:1, :], scalar=msk[0:1, 1:2],
                in1=km16[0:1, :], op0=ALU.mult, op1=ALU.add),
                inc="pv")

        psq_war = [None, None]
        q2s = {"ev2": q2s_seed[0], "qsq_war": q2s_seed[1], "tmx": None,
               "pending": None}

        def flush_q2():
            if q2s["pending"] is not None:
                q2s["pending"]()
                q2s["pending"] = None

        def make_q2(h, ev):
            def emit():
                tq = p.op("scalar", lambda e, h=h: e.activation(
                    qsq[:], q_bf[:, h * TOK:(h + 1) * TOK], AF.Square),
                    wait=(ev, q2s["qsq_war"]), inc="pa")
                for sl in range(NQT):
                    mm2 = p.op("tensor", lambda e, sl=sl: e.matmul(
                        ps_row[:], ones_b[:, 0:1], qsq[:, sl * 512:(sl + 1) * 512],
                        start=True, stop=True), wait=(tq, q2s["ev2"]), inc="pt")
                    q2s["ev2"] = p.op("vector", lambda e, sl=sl: e.tensor_copy(
                        rowbuf2[0:1, sl * 512:(sl + 1) * 512], ps_row[:]),
                        wait=(mm2,), inc="pv")
                    q2s["qsq_war"] = mm2
                q2s["tmx"] = p.op("vector", lambda e, h=h: e.reduce_max(
                    qmax2[0:1, h:h + 1], rowbuf2[0:1, 0:TOK], axis=AX.X),
                    wait=(q2s["ev2"],), inc="pv")
            return emit

        for os_ in range(4):
            t_slab = slab_toks[os_]
            for oc4 in range(4):
                h = os_ * 4 + oc4
                ev = None
                for qt in range(NQT):
                    u = (oc4 * NQT + qt) % 2
                    mm = None
                    for dc in range(NDC):
                        w = (t_slab, t_xn1own, psq_war[u]) if dc == 0 else ()
                        mm = p.op("tensor",
                                  lambda e, os_=os_, oc4=oc4, qt=qt, dc=dc, u=u:
                                  e.matmul(
                            ps_q[u][:],
                            wqs[os_ % 3][:, dc * 512 + oc4 * 128:
                                         dc * 512 + oc4 * 128 + 128],
                            xn1T_own[:, dc * TOK + qt * 512:
                                     dc * TOK + (qt + 1) * 512],
                            start=(dc == 0), stop=(dc == NDC - 1)),
                            wait=w, inc=("pt" if dc == NDC - 1 else None))
                    ev = p.op("vector", lambda e, h=h, qt=qt, u=u: e.tensor_copy(
                        q_bf[:, h * TOK + qt * 512: h * TOK + (qt + 1) * 512],
                        ps_q[u][:]), wait=(mm,), inc="pv")
                    psq_war[u] = ev
                    if qt == 0:
                        flush_q2()
                q2s["pending"] = make_q2(h, ev)
            slab_done[os_] = mm
            if os_ == 0:
                war = mm
                for dc in range(NDC):
                    fn = (lambda e, dc=dc: e.dma_start(
                        out=wqs[0][:, dc * 512:(dc + 1) * 512],
                        in_=Wq_d[dc * 128:(dc + 1) * 128, 1536:2048]))
                    p.cnt["q0"] += 16
                    tok3 = ("q0", p.cnt["q0"])
                    p.ops.insert(slab3_dma_slot + dc,
                                 ("gpsimd", fn,
                                  (war,) if dc == 0 else (), "q0", 16))
                slab_toks.append(tok3)
        flush_q2()
        t_qmax = q2s["tmx"]
        t_kvbl = emit_kv_blend()
        t_kmax = p.op("vector", lambda e: e.reduce_max(
            kmax2[0:1, 0:4],
            km16[0:1, :].rearrange("p (r g) -> p g r", g=4), axis=AX.X),
            wait=(t_kvbl,), inc="pv")
        p.barrier()
        p.mark("B1")
        dbg("K", K_sb, [128, G * S], BF16)
        dbg("VT", VT_sb, [128, NKC * RANK], BF16)
        dbg("q", q_bf, [128, H * TOK], BF16)
        sfree("wqs0", "wqs1", "wqs2", "wqs3", "qsq", "xn1T_own",
              "ps_q0", "ps_q1")

        negc0_bf = salloc("negc0_bf", [1, 16], BF16, side="right")
        negc0T = salloc("negc0T", [128, 16], F32, side="right")
        ps_c0 = palloc("ps_c0", [128, 16], F32)
        c0row = rowbuf[0:1, 0:16]
        km3 = kmax2[0:1, 0:4].unsqueeze(2).broadcast_to([1, 4, 4])
        qm3 = qmax2[0:1, :].rearrange("p (a b) -> p a b", b=4)
        c3v = c0row.rearrange("p (a b) -> p a b", b=4)
        t1 = p.op("vector", lambda e: e.tensor_tensor(
            c3v, qm3, km3, ALU.add), wait=(t_kmax, t_qmax), inc="pv")
        t2 = p.op("vector", lambda e: e.tensor_scalar(
            out=negc0_bf[0:1, :], in0=c0row,
            scalar1=-0.5 * SCALE, scalar2=None, op0=ALU.mult),
            wait=(t1,), inc="pv")
        t3 = p.op("tensor", lambda e: e.matmul(
            ps_c0[:], ones_b[0:1, :], negc0_bf[0:1, :], start=True, stop=True),
            wait=(t2,), inc="pt")
        t_negc0 = p.op("vector", lambda e: e.tensor_copy(
            negc0T[:], ps_c0[:]), wait=(t3,), inc="pv")
        p.barrier()
        p.mark("B2")
        dbg("kmax2", kmax2, [1, 8], F32)
        dbg("qmax2", qmax2, [1, 16], F32)
        dbg("negc0T", negc0T, [128, 16], F32)
        sfree("ps_c0", "ps_tr0", "ps_tr1", "ps_mm0", "ps_mm1", "ps_row",
              "rowbuf", "km16", "km16b", "kmax2", "qmax2")

        attn_fm = salloc("attn_fm", [128, H * TOK], BF16)
        PT = salloc("PT", [128, 4 * 512], BF16)
        ibc = [salloc(f"ibc{i}", [128, 512], F32) for i in range(2)]
        ps_s = [palloc(f"ps_s{i}", [128, 1024], F32) for i in range(2)]
        ps_pv = [palloc(f"ps_pv{i}", [128, 512], F32) for i in range(2)]
        ps_sb = [palloc(f"ps_sb{i}", [128, 512], F32) for i in range(2)]

        NG = NKC // 2
        unit_done = [t_negc0, t_negc0]
        last_pv = [None, None]
        uidx = 0
        pend_pe = [None]

        def flush_pe():
            if pend_pe[0] is not None:
                pend_pe[0]()
                pend_pe[0] = None

        for h in range(H):
            g = h // 4
            for qt in range(NQT):
                pi = uidx % 2
                tok_sc = [None] * NG
                tok_exp = [None] * NG
                tok_pv = [None] * NG

                def emit_scores(i):
                    tok = None
                    for half in range(2):
                        kc = i * 2 + half
                        tok = p.op("tensor",
                                   lambda e, i=i, half=half, kc=kc, h=h, qt=qt, g=g:
                                   e.matmul(
                            ps_s[i % 2][:, half * 512:(half + 1) * 512],
                            K_sb[:, g * S + kc * 128: g * S + kc * 128 + 128],
                            q_bf[:, h * TOK + qt * 512: h * TOK + (qt + 1) * 512],
                            start=True, stop=True, skip_group_check=True),
                            inc=("pt" if half == 1 else None))
                    tok_sc[i] = tok

                def emit_exp(i):
                    w = [tok_sc[i], last_pv[i % 2]]
                    tok_exp[i] = p.op("scalar", lambda e, i=i, h=h: e.activation(
                        PT[:, (i % 2) * 1024:(i % 2) * 1024 + 1024],
                        ps_s[i % 2][:], AF.Exp,
                        bias=negc0T[:, h:h + 1], scale=SCALE),
                        wait=tuple(w), inc="pa")

                def emit_pv(i):
                    tok = None
                    for half in range(2):
                        kc = i * 2 + half
                        slot = PT[:, (i % 2) * 1024 + half * 512:
                                  (i % 2) * 1024 + (half + 1) * 512]
                        w = [tok_exp[i]] if half == 0 else []
                        if i == 0 and half == 0:
                            w.append(unit_done[pi])
                        p.op("tensor", lambda e, kc=kc, g=g, slot=slot, pi=pi:
                             e.matmul(
                            ps_pv[pi][:],
                            VT_sb[:, kc * RANK + g * 128: kc * RANK + g * 128 + 128],
                            slot, start=(kc == 0), stop=(kc == NKC - 1),
                            skip_group_check=True), wait=tuple(w))
                        tok = p.op("tensor", lambda e, kc=kc, slot=slot, pi=pi:
                                   e.matmul(
                            ps_sb[pi][0:1, :], ones_b[:, 0:1], slot,
                            start=(kc == 0), stop=(kc == NKC - 1),
                            skip_group_check=True),
                            inc=("pt" if half == 1 else None))
                    tok_pv[i] = tok
                    last_pv[i % 2] = tok

                emit_scores(0)
                for i in range(1, NG):
                    emit_scores(i)
                    emit_exp(i - 1)
                    emit_pv(i - 1)
                    if i == 3:
                        flush_pe()
                emit_exp(NG - 1)
                emit_pv(NG - 1)

                t_se = p.op("vector", lambda e, pi=pi: e.tensor_copy(
                    rowbuf2[0:1, pi * 1024:pi * 1024 + 512], ps_sb[pi][0:1, :]),
                    wait=(tok_pv[NG - 1],), inc="pv")
                t_ri = p.op("vector", lambda e, pi=pi: e.reciprocal(
                    rowbuf2[0:1, pi * 1024 + 512:pi * 1024 + 1024],
                    rowbuf2[0:1, pi * 1024:pi * 1024 + 512]),
                    wait=(t_se,), inc="pv")

                def mk_tail(h=h, qt=qt, pi=pi, t_ri=t_ri):
                    def emit():
                        t_bc = p.op("tensor", lambda e, pi=pi: e.matmul(
                            ps_sb[pi][:], ones_f[0:1, :],
                            rowbuf2[0:1, pi * 1024 + 512:pi * 1024 + 1024],
                            start=True, stop=True, skip_group_check=True),
                            wait=(t_ri,), inc="pt")
                        t_ie = p.op("vector", lambda e, pi=pi: e.tensor_copy(
                            ibc[pi][:], ps_sb[pi][:]), wait=(t_bc,), inc="pv")
                        unit_done[pi] = p.op(
                            "vector", lambda e, h=h, qt=qt, pi=pi: e.tensor_tensor(
                                attn_fm[:, h * TOK + qt * 512:
                                        h * TOK + (qt + 1) * 512],
                                ps_pv[pi][:], ibc[pi][:], ALU.mult),
                            wait=(t_ie,), inc="pv")
                    return emit
                pend_pe[0] = mk_tail()
                uidx += 1
        flush_pe()
        tok_unit_done = unit_done[(uidx - 1) % 2]

        p.barrier()
        p.mark("B3")
        dbg("attn", attn_fm, [128, H * TOK], BF16)
        sfree("q_bf", "K_sb", "VT_sb", "PT", "ibc0", "ibc1", "negc0_bf",
              "negc0T", "rowbuf2", "ps_s0", "ps_s1", "ps_pv0", "ps_pv1",
              "ps_sb0", "ps_sb1")

        xo2 = salloc("xo2", [128, NTC * DM], F32)
        wos = [salloc(f"wos{i}", [128, H * 512], BF16) for i in range(2)]
        ps_o = [palloc(f"ps_o{i}", [128, 512], F32) for i in range(2)]

        for tc in range(NTC):
            p.dma("scalar", xo2[:, tc * DM:(tc + 1) * DM],
                  x_own_d[tc * 128:(tc + 1) * 128, :], inc="aq")
        t_ld2 = ("aq", p.cnt["aq"])
        ld2 = [t_ld2] * NTC
        wos_war = [None, None]
        pso_war = [None, None]
        for os_ in range(4):
            t_slab = None
            for cc in range(H):
                t_slab = p.dma(
                    "gpsimd", wos[os_ % 2][:, cc * 512:(cc + 1) * 512],
                    Wo_d[cc * 128:(cc + 1) * 128, os_ * 512:(os_ + 1) * 512],
                    wait=((wos_war[os_ % 2],) if cc == 0 else ()),
                    inc=f"o{os_ % 2}")
            last_mm = None
            for tc in range(NTC):
                mm = None
                for cc in range(H):
                    w = (t_slab, pso_war[tc % 2]) if cc == 0 else ()
                    mm = p.op("tensor", lambda e, os_=os_, tc=tc, cc=cc: e.matmul(
                        ps_o[tc % 2][:],
                        attn_fm[:, cc * TOK + tc * 128: cc * TOK + tc * 128 + 128],
                        wos[os_ % 2][:, cc * 512:(cc + 1) * 512],
                        start=(cc == 0), stop=(cc == H - 1)),
                        wait=w, inc=("pt" if cc == H - 1 else None))
                ev = p.op("vector", lambda e, os_=os_, tc=tc: e.tensor_tensor(
                    xo2[:, tc * DM + os_ * 512: tc * DM + (os_ + 1) * 512],
                    ps_o[tc % 2][:],
                    xo2[:, tc * DM + os_ * 512: tc * DM + (os_ + 1) * 512],
                    ALU.add), wait=(mm, ld2[tc]), inc="pv")
                pso_war[tc % 2] = ev
                last_mm = mm
            wos_war[os_ % 2] = last_mm
        p.barrier()
        p.mark("B4")
        dbg("x1", xo2, [128, NTC * DM], F32)
        sfree("attn_fm", "wos0", "wos1", "ps_o0", "ps_o1")

        xn2T = salloc("xn2T", [128, NDC * TOK], BF16)
        xnb3 = [salloc(f"c_xnb{i}", [128, DM], BF16) for i in range(2)]
        sq_scr2 = salloc("sq_scr2", [128, DM], F32)
        ps_tr = [palloc(f"ps_tr{i}", [128, 128], BF16) for i in range(2)]
        sts = []
        for tc in range(NTC):
            t1, t2 = ln_stats_chunk(xo2[:, tc * DM:(tc + 1) * DM], tc, None,
                                    sq_scr2)
            sts += [t1, t2]
        t_stats = ln_finish(NTC, sts)
        xnb3_war = [None, None]
        tr_war = [None, None]
        t_xn2 = None
        for tc in range(NTC):
            nb = xnb3[tc % 2]
            ta = p.op("vector", lambda e, nb=nb, tc=tc: e.tensor_scalar(
                out=nb[:], in0=xo2[:, tc * DM:(tc + 1) * DM],
                scalar1=sm[:, tc:tc + 1], scalar2=sm[:, 64 + tc:65 + tc],
                op0=ALU.subtract, op1=ALU.mult),
                wait=(t_stats, xnb3_war[tc % 2]), inc="pv")
            t_xn2 = ln_transpose_chunk(nb, xn2T, TOK, tc, ta, tr_war)
            xnb3_war[tc % 2] = t_xn2
        p.barrier()
        p.mark("C0")
        dbg("xn2T", xn2T, [128, NDC * TOK], BF16)
        sfree("c_xnb0", "c_xnb1", "ps_tr0", "ps_tr1", "sq_scr2")

        h_sb = salloc("h_sb", [128, NHC * 512], BF16)
        QW = 256
        NQUAD = HF // QW
        wgt = [salloc(f"wgt{i}", [128, NDC * QW], BF16) for i in range(4)]
        silu_t = [salloc("silu_t0", [128, 512], BF16, side="right")] * 2
        h2_t = [salloc(f"h2_t{i}", [128, 512], BF16, side="right") for i in range(2)]
        wpt = [salloc(f"wpt{i}", [128, 1024], BF16, side="right") for i in range(3)]
        ssqT = salloc("ssqT", [128, 8], F32, side="right")
        r2T = salloc("r2T", [128, 8], F32, side="right")
        rmsrow = salloc("rmsrow", [1, 512], F32, side="right")

        wguT_all_ap = wguT_all.ap()
        wpT_all_v = wpT_all.ap().rearrange("(r hh) o -> hh r o", r=NCORES)

        def gu_tile_ap(quad, is_up):
            hglob = (HF if is_up else 0) + quad * QW
            r = hglob // GSH
            hl = hglob % GSH
            base = wguT_all_ap[r * DM:(r + 1) * DM, hl:hl + QW]
            return base.rearrange("(dc pp) h4 -> pp dc h4", pp=128)

        wgt_war = [None, None]
        ps_c2_war = [None] * 8
        for th in range(2):
            ps_g = [palloc(f"ps_g{i}", [128, 512], F32) for i in range(2)]
            ps_u = [palloc(f"ps_u{i}", [128, 512], F32) for i in range(2)]
            ps_rms = palloc("ps_rms", [1, 512], F32)
            ps_trf = palloc("ps_trf", [128, 4], F32)
            h_done = [None] * NHC
            rms_mms = [None] * NHC
            t2hs = [None] * NHC
            rms_mm = None
            for quad in range(NQUAD):
                w = [wgt_war[quad % 2]]
                if quad == 0:
                    w += [t_cc1, t_xn2]
                tg = tu = None
                for piece in range(4):
                    ds = slice(piece * 4 * QW, (piece + 1) * 4 * QW)
                    ss = slice(piece * 4, (piece + 1) * 4)
                    tg = p.dma("sync", wgt[(quad % 2) * 2][:, ds].rearrange(
                        "p (dc h4) -> p dc h4", h4=QW),
                        gu_tile_ap(quad, False)[:, ss, :],
                        wait=tuple(w) if piece == 0 else (),
                        inc=f"g{quad % 2}")
                for piece in range(4):
                    ds = slice(piece * 4 * QW, (piece + 1) * 4 * QW)
                    ss = slice(piece * 4, (piece + 1) * 4)
                    tu = p.dma("sync", wgt[(quad % 2) * 2 + 1][:, ds].rearrange(
                        "p (dc h4) -> p dc h4", h4=QW),
                        gu_tile_ap(quad, True)[:, ss, :],
                        inc=f"g{quad % 2}")
                for k in range(QW // 128):
                    pr = quad * (QW // 128) + k
                    pg, pu = ps_g[pr % 2], ps_u[pr % 2]
                    mmg = mmu = None
                    for dc in range(NDC):
                        wmm = (tg, tu, h_done[pr - 2] if pr >= 2 else None
                               ) if dc == 0 else ()
                        mmg = p.op("tensor",
                                   lambda e, quad=quad, k=k, dc=dc, th=th, pg=pg:
                                   e.matmul(
                            pg[:],
                            wgt[(quad % 2) * 2][:, dc * QW + k * 128:
                                                dc * QW + k * 128 + 128],
                            xn2T[:, dc * TOK + th * 512: dc * TOK + (th + 1) * 512],
                            start=(dc == 0), stop=(dc == NDC - 1),
                            skip_group_check=True),
                            wait=wmm, inc=("pt" if dc == NDC - 1 else None))
                    for dc in range(NDC):
                        mmu = p.op("tensor",
                                   lambda e, quad=quad, k=k, dc=dc, th=th, pu=pu:
                                   e.matmul(
                            pu[:],
                            wgt[(quad % 2) * 2 + 1][:, dc * QW + k * 128:
                                                    dc * QW + k * 128 + 128],
                            xn2T[:, dc * TOK + th * 512: dc * TOK + (th + 1) * 512],
                            start=(dc == 0), stop=(dc == NDC - 1),
                            skip_group_check=True),
                            inc=("pt" if dc == NDC - 1 else None))
                    tsl = p.op("scalar", lambda e, pg=pg, pr=pr: e.activation(
                        silu_t[pr % 2][:], pg[:], AF.Silu),
                        wait=(mmg, mmu, h_done[pr - 1] if pr >= 1 else None),
                        inc="pa")
                    if pr >= 1:
                        rms_mm = p.op("tensor", lambda e, pr=pr: e.matmul(
                            ps_rms[:], ones_b[:, 0:1], h2_t[(pr - 1) % 2][:],
                            start=(pr - 1 == 0), stop=False,
                            skip_group_check=True), wait=(t2hs[pr - 1],),
                            inc="pt")
                        rms_mms[pr - 1] = rms_mm
                    th_tok = p.op("vector", lambda e, pr=pr, pu=pu: e.tensor_tensor(
                        h_sb[:, pr * 512:(pr + 1) * 512], pu[:],
                        silu_t[pr % 2][:], ALU.mult), wait=(tsl,), inc="pv")
                    t2h = p.op("vector", lambda e, pr=pr: e.tensor_tensor(
                        h2_t[pr % 2][:], h_sb[:, pr * 512:(pr + 1) * 512],
                        h_sb[:, pr * 512:(pr + 1) * 512], ALU.mult),
                        wait=(th_tok, rms_mms[pr - 2] if pr >= 2 else None),
                        inc="pv")
                    t2hs[pr] = t2h
                    if pr == NHC - 1:
                        rms_mm = p.op("tensor", lambda e, pr=pr: e.matmul(
                            ps_rms[:], ones_b[:, 0:1], h2_t[pr % 2][:],
                            start=False, stop=True,
                            skip_group_check=True), wait=(t2h,), inc="pt")
                        rms_mms[pr] = rms_mm
                    h_done[pr] = th_tok
                wgt_war[quad % 2] = rms_mm
            t_re = p.op("vector", lambda e: e.tensor_copy(
                rmsrow[0:1, :], ps_rms[:]), wait=(rms_mm,), inc="pv")
            ttf = None
            for tc4 in range(4):
                ttf = p.op("tensor", lambda e, tc4=tc4: e.transpose(
                    ps_trf[:, tc4:tc4 + 1],
                    rmsrow[0:1, tc4 * 128:(tc4 + 1) * 128], id_f[0:1, 0:1]),
                    wait=(t_re,), inc="pt")
            cs = th * 4
            tev = p.op("vector", lambda e, cs=cs: e.tensor_copy(
                ssqT[:, cs:cs + 4], ps_trf[:]), wait=(ttf,), inc="pv")
            tss = p.op("vector", lambda e, cs=cs: e.tensor_scalar(
                out=ssqT[:, cs:cs + 4], in0=ssqT[:, cs:cs + 4],
                scalar1=1.0 / HF, scalar2=EPS_RMS, op0=ALU.mult, op1=ALU.add),
                wait=(tev,), inc="pv")
            tsq = p.op("scalar", lambda e, cs=cs: e.activation(
                r2T[:, cs:cs + 4], ssqT[:, cs:cs + 4], AF.Sqrt),
                wait=(tss,), inc="pa")
            trc = p.op("vector", lambda e, cs=cs: e.reciprocal(
                r2T[:, cs:cs + 4], r2T[:, cs:cs + 4]), wait=(tsq,), inc="pv")
            p.op("vector", lambda e, cs=cs: e.tensor_tensor(
                sm[:, 0:4], r2T[:, cs:cs + 4], r2T[:, cs:cs + 4], ALU.mult),
                wait=(trc,), inc="pv")
            p.op("vector", lambda e, cs=cs: e.tensor_tensor(
                sm[:, 0:4], sm[:, 0:4], ssqT[:, cs:cs + 4], ALU.mult), inc="pv")
            p.op("vector", lambda e: e.tensor_scalar(
                out=sm[:, 0:4], in0=sm[:, 0:4],
                scalar1=-0.5, scalar2=1.5, op0=ALU.mult, op1=ALU.add), inc="pv")
            t_r2 = p.op("vector", lambda e, cs=cs: e.tensor_tensor(
                r2T[:, cs:cs + 4], sm[:, 0:4], r2T[:, cs:cs + 4], ALU.mult),
                inc="pv")
            p.barrier()
            if th == 0:
                dbg("h0", h_sb, [128, NHC * 512], BF16)
                dbg("r2T", r2T, [128, 8], F32)
                dbg("ssqT", ssqT, [128, 8], F32)
            sfree("ps_g0", "ps_g1", "ps_u0", "ps_u1", "ps_rms", "ps_trf")

            ps_c2 = [palloc(f"ps_c2_{u}", [128, 512], F32) for u in range(8)]
            wpt_war = [None] * 3
            for oh in range(2):
                mm_final = [None] * 8
                for hc in range(NHC):
                    w = [wpt_war[hc % 3]]
                    if hc == 0:
                        w.append(t_cc2)
                    src = wpT_all_v[hc * 128:(hc + 1) * 128,
                                    oh * 4:(oh + 1) * 4, :]
                    tw = p.dma("sync", wpt[hc % 3][:].rearrange(
                        "p (r o) -> p r o", o=OSH), src, wait=tuple(w),
                        inc=f"w{hc % 3}")
                    for u in range(8):
                        tc4, o5 = u // 2, u % 2
                        if hc == 0:
                            wmm = (tw, h_done[NHC - 1], ps_c2_war[u])
                        elif u == 0:
                            wmm = (tw,)
                        else:
                            wmm = ()
                        mm_final[u] = p.op(
                            "tensor", lambda e, hc=hc, u=u, tc4=tc4, o5=o5:
                            e.matmul(
                                ps_c2[u][:],
                                h_sb[:, hc * 512 + tc4 * 128:
                                     hc * 512 + tc4 * 128 + 128],
                                wpt[hc % 3][:, o5 * 512:(o5 + 1) * 512],
                                start=(hc == 0), stop=(hc == NHC - 1),
                                skip_group_check=True),
                            wait=wmm,
                            inc=("pt" if (hc == NHC - 1 or u == 7) else None))
                    wpt_war[hc % 3] = mm_final[7]
                for u in range(8):
                    tc4, o5 = u // 2, u % 2
                    tcg = th * 4 + tc4
                    col = tcg * DM + oh * 1024 + o5 * 512
                    ps_c2_war[u] = p.op(
                        "vector",
                        lambda e, u=u, tcg=tcg, col=col: e.scalar_tensor_tensor(
                            out=xo2[:, col:col + 512], in0=ps_c2[u][:],
                            scalar=r2T[:, tcg:tcg + 1],
                            in1=xo2[:, col:col + 512],
                            op0=ALU.mult, op1=ALU.add),
                        wait=(mm_final[u], t_r2), inc="pv")
            p.barrier()
            if th == 0:
                dbg("x2a", xo2, [128, NTC * DM], F32)
                if "wpT" in DEBUG_DUMPS:
                    dwp = nc.dram_tensor("dbg_wpT", [1024, OSH], BF16,
                                         kind="ExternalOutput").ap()
                    p.dma("scalar", dwp, wpT_all.ap()[0:1024, :], inc="sq")
            sfree(*[f"ps_c2_{u}" for u in range(8)])

        for tc in range(NTC):
            p.dma("scalar", out_d[tc * 128:(tc + 1) * 128, :],
                  xo2[:, tc * DM:(tc + 1) * DM], inc="sq")
        p.op("sync", lambda e: e.nop(), wait=(("sq", p.cnt["sq"]),))

        p.mark("END")
        p.finalize()

        with nc.Block() as block:
            @block.tensor
            def _(eng):
                p.run(eng, "tensor")

            @block.vector
            def _(eng):
                p.run(eng, "vector")

            @block.scalar
            def _(eng):
                p.run(eng, "scalar")

            @block.gpsimd
            def _(eng):
                p.run(eng, "gpsimd")

            @block.sync
            def _(eng):
                p.run(eng, "sync")

    return nc


def kernel(L0_attn_hidden_states, L0_attn_Wq, L0_attn_Wc, L0_attn_Wup,
           L0_attn_Wo, L0_ffn_w_gate_up, L0_ffn_w_proj):
    x = np.ascontiguousarray(np.asarray(L0_attn_hidden_states, dtype=np.float32))
    Wq = np.ascontiguousarray(np.asarray(L0_attn_Wq, dtype=np.float32))
    Wc = np.ascontiguousarray(np.asarray(L0_attn_Wc, dtype=np.float32))
    Wup = np.ascontiguousarray(np.asarray(L0_attn_Wup, dtype=np.float32))
    Wo = np.ascontiguousarray(np.asarray(L0_attn_Wo, dtype=np.float32))
    wgu = np.ascontiguousarray(np.asarray(L0_ffn_w_gate_up, dtype=np.float32))
    wp = np.ascontiguousarray(np.asarray(L0_ffn_w_proj, dtype=np.float32))

    ones_b = np.ones((128, 128), dtype=bfdt)
    id_b = np.eye(128, dtype=bfdt)
    id_f = np.eye(128, dtype=np.float32)
    ones_f = np.ones((1, 128), dtype=np.float32)

    nc = build_program()
    in_maps = []
    for c in range(NCORES):
        b = c // 4
        t0 = (c % 4) * TOK
        mask = np.zeros((128, 2), dtype=np.float32)
        mask[:, b] = 1.0
        in_maps.append({
            "x_own": np.ascontiguousarray(x[b, t0:t0 + TOK]),
            "Wq": Wq, "Wc": Wc, "Wup": Wup, "Wo": Wo,
            "wgu_sh": np.ascontiguousarray(wgu[c * GSH:(c + 1) * GSH]),
            "wp_sh": np.ascontiguousarray(wp[c * OSH:(c + 1) * OSH]),
            "ones_b": ones_b, "id_b": id_b, "id_f": id_f, "ones_f": ones_f,
            "mask": mask,
        })
    res = run_bass_kernel_spmd(nc, in_maps, core_ids=list(range(NCORES)))
    out = np.empty((B, S, DM), dtype=np.float32)
    for c in range(NCORES):
        b = c // 4
        t0 = (c % 4) * TOK
        out[b, t0:t0 + TOK] = res.results[c]["out"]
    return out


# version 18 (final)
# speedup vs baseline: 1.0146x; 1.0146x over previous
import numpy as np
import ml_dtypes
from contextlib import ExitStack

import concourse.bass as bass
import concourse.mybir as mybir
from concourse.bass_utils import run_bass_kernel_spmd

F32 = mybir.dt.float32
BF16 = mybir.dt.bfloat16
AF = mybir.ActivationFunctionType
ALU = mybir.AluOpType
AX = mybir.AxisListType
bfdt = ml_dtypes.bfloat16

NCORES = 8
B, S, DM = 2, 4096, 2048
H, DH, G, RANK = 16, 128, 4, 512
NDC = DM // 128
TOK = 1024
NTC = TOK // 128
NQT = TOK // 512
NKC = S // 128
SCALE = float(DH) ** -0.5
HF = 8192
NHC = HF // 128
EPS_LN = 1e-5
EPS_RMS = float(np.finfo(np.float32).eps)
SIG1 = float(1.0 / (1.0 + np.exp(-1.0)))
GSH = 2048
OSH = 256

ENGS = ("tensor", "vector", "scalar", "gpsimd", "sync")


class _StopBuild(Exception):
    pass

DEBUG_DUMPS = []
STOP_PHASE = None


CHAIN_SEM = {"vector": "pv", "scalar": "pa", "gpsimd": "pg"}


class Plan:
    def __init__(self, nc, ctx):
        self.nc = nc
        self.ctx = ctx
        self.ops = []
        self.cnt = {}
        self.h = {}
        self.chain = {"vector": None, "scalar": None, "gpsimd": None}

    def sem(self, name):
        self.h[name] = self.ctx.enter_context(self.nc.semaphore(name))
        self.cnt[name] = 0

    def op(self, eng, fn, wait=(), inc=None, n=1, is_dma=False):
        wait = [w for w in wait if w is not None]
        if eng in CHAIN_SEM and not is_dma:
            if inc is None:
                inc = CHAIN_SEM[eng]
            if self.chain[eng] is not None:
                wait.append(self.chain[eng])
        tok = None
        if inc is not None:
            self.cnt[inc] += n
            tok = (inc, self.cnt[inc])
        if eng in CHAIN_SEM and not is_dma:
            self.chain[eng] = tok
        self.ops.append((eng, fn, tuple(wait), inc, n))
        return tok

    def dma(self, eng, out, in_, wait=(), inc=None):
        return self.op(eng, (lambda e, o=out, i=in_: e.dma_start(out=o, in_=i)),
                       wait, inc, 16, is_dma=True)

    def barrier(self):
        waits = tuple((s, c) for s, c in self.cnt.items()
                      if c > 0 and not s.startswith("cc"))
        for e in ENGS:
            self.op(e, lambda eng: eng.nop(), wait=waits)

    def mark(self, name):
        if STOP_PHASE == name and not hasattr(self, "trunc"):
            self.trunc = (len(self.ops), dict(self.cnt))

    def finalize(self):
        if hasattr(self, "trunc"):
            idx, cnt = self.trunc
            self.ops = self.ops[:idx]
            waits = tuple((s, c) for s, c in cnt.items() if c > 0)
            for e in ENGS:
                self.ops.append((e, lambda eng: eng.nop(), waits, None, 1))

    def run(self, engobj, name):
        seen = {}
        for (e, fn, waits, inc, n) in self.ops:
            if e != name:
                continue
            for (s, v) in waits:
                if seen.get(s, -1) < v:
                    engobj.wait_ge(self.h[s], v)
                    seen[s] = v
            ins = fn(engobj)
            if inc is not None:
                ins.then_inc(self.h[inc], n)


def build_program():
    nc = bass.Bass("TRN2", target_bir_lowering=False)

    x_own_d = nc.dram_tensor("x_own", [TOK, DM], F32, kind="ExternalInput").ap()
    Wq_d = nc.dram_tensor("Wq", [DM, DM], F32, kind="ExternalInput").ap()
    Wc_d = nc.dram_tensor("Wc", [DM, RANK], F32, kind="ExternalInput").ap()
    Wup_d = nc.dram_tensor("Wup", [RANK, RANK], F32, kind="ExternalInput").ap()
    Wo_d = nc.dram_tensor("Wo", [DM, DM], F32, kind="ExternalInput").ap()
    wgu_d = nc.dram_tensor("wgu_sh", [GSH, DM], F32, kind="ExternalInput").ap()
    wp_d = nc.dram_tensor("wp_sh", [OSH, HF], F32, kind="ExternalInput").ap()
    ones_b_d = nc.dram_tensor("ones_b", [128, 128], BF16, kind="ExternalInput").ap()
    id_b_d = nc.dram_tensor("id_b", [128, 128], BF16, kind="ExternalInput").ap()
    id_f_d = nc.dram_tensor("id_f", [128, 128], F32, kind="ExternalInput").ap()
    ones_f_d = nc.dram_tensor("ones_f", [1, 128], F32, kind="ExternalInput").ap()
    mask_d = nc.dram_tensor("mask", [128, 2], F32, kind="ExternalInput").ap()
    out_d = nc.dram_tensor("out", [TOK, DM], F32, kind="ExternalOutput").ap()

    wguT_loc = nc.dram_tensor("wguT_loc", [DM, GSH], BF16)
    wpT_loc = nc.dram_tensor("wpT_loc", [HF, OSH], BF16)
    wguT_all = nc.dram_tensor("wguT_all", [NCORES * DM, GSH], BF16)
    wpT_all = nc.dram_tensor("wpT_all", [NCORES * HF, OSH], BF16)
    kv_gin = nc.dram_tensor("kv_gin", [128, 2 * G * TOK + 256], BF16)
    kv_gall = nc.dram_tensor("kv_gall", [NCORES * 128, 2 * G * TOK + 256], BF16)

    with ExitStack() as ctx:
        p = Plan(nc, ctx)
        for s in ("dq", "aq", "gq", "sq", "pt", "pv", "pa", "pg",
                  "cc1", "cc2", "cc3", "wc", "g0", "g1", "w0", "w1", "w2",
                  "o0", "o1", "q0", "q1", "q2", "a0", "a1", "a2", "a3",
                  "t0", "t1", "s0"):
            p.sem(s)

        lo = (nc._sbuf_addr_for_side("left") + 31) & ~31
        hi = nc._sbuf_addr_for_side("right")
        arena_free = [(lo, hi)]
        sb_map = {}
        ps_live = {}
        uid = [0]

        def _arena_alloc(nbytes, side="left"):
            nbytes = (nbytes + 31) & ~31
            order = range(len(arena_free)) if side == "left" else \
                range(len(arena_free) - 1, -1, -1)
            for idx in order:
                a, b = arena_free[idx]
                if b - a >= nbytes:
                    if side == "left":
                        addr = a
                        arena_free[idx] = (a + nbytes, b)
                    else:
                        addr = b - nbytes
                        arena_free[idx] = (a, b - nbytes)
                    if arena_free[idx][0] == arena_free[idx][1]:
                        arena_free.pop(idx)
                    return addr, nbytes
            raise MemoryError(f"SBUF arena full ({nbytes}B)")

        def _arena_release(a, n):
            arena_free.append((a, a + n))
            arena_free.sort()
            i = 0
            while i + 1 < len(arena_free):
                x, y = arena_free[i], arena_free[i + 1]
                if x[1] == y[0]:
                    arena_free[i] = (x[0], y[1])
                    arena_free.pop(i + 1)
                else:
                    i += 1

        def salloc(name, shape, dtype, side="left"):
            per = int(np.prod(shape[1:])) * (4 if dtype == F32 else 2)
            try:
                addr, n = _arena_alloc(per, side)
            except MemoryError:
                used = sorted(sb_map.items(), key=lambda kv: kv[1][0])
                raise MemoryError(
                    f"SBUF full allocating {name} ({per}B); live: "
                    + ", ".join(f"{k}@{a}+{s}" for k, (a, s) in used))
            sb_map[name] = (addr, n)
            uid[0] += 1
            return nc.alloc_sbuf_tensor_at(f"{name}_u{uid[0]}", shape, dtype,
                                           offset=addr)

        def palloc(name, shape, dtype=F32):
            uid[0] += 1
            g = nc.psum_tensor(f"{name}_u{uid[0]}", shape, dtype)
            t = g.__enter__()
            ps_live[name] = (g, uid[0])
            return t

        def sfree(*names):
            ps = [n for n in names if n in ps_live]
            for n in sorted(ps, key=lambda n: -ps_live[n][1]):
                ps_live.pop(n)[0].__exit__(None, None, None)
            for n in names:
                if n in sb_map:
                    a, sz = sb_map.pop(n)
                    _arena_release(a, sz)

        def dbg(tag, t, shape, dtype):
            if tag not in DEBUG_DUMPS:
                return
            d = nc.dram_tensor(f"dbg_{tag}", shape, dtype,
                               kind="ExternalOutput").ap()
            p.dma("scalar", d, t[:, 0:shape[1]], inc="sq")

        ones_b = salloc("ones_b_s", [128, 128], BF16)
        id_b = salloc("id_b_s", [128, 128], BF16)
        id_f = salloc("id_f_s", [128, 128], F32)
        ones_f = salloc("ones_f_s", [1, 128], F32)
        msk = salloc("msk_s", [128, 2], F32)
        sm = salloc("sm", [128, 128], F32, side="right")
        rowbuf = salloc("rowbuf", [1, 1024], F32, side="right")

        p.dma("sync", ones_b[:], ones_b_d, inc="dq")
        p.dma("sync", id_b[:], id_b_d, inc="dq")
        p.dma("sync", id_f[:], id_f_d, inc="dq")
        p.dma("sync", msk[:], mask_d, inc="dq")
        t_const = p.dma("sync", ones_f[:], ones_f_d, inc="dq")
        for e in ("tensor", "vector", "scalar", "gpsimd"):
            p.op(e, lambda eng: eng.nop(), wait=(t_const,))

        wt = [salloc(f"q_wt{i}", [128, HF], BF16) for i in range(2)]
        xq = salloc("q_xq", [128, HF], BF16)
        abuf = salloc("q_a", [128, HF], BF16)
        bbuf = salloc("q_b", [128, HF], BF16)
        slabT = salloc("q_slab", [128, HF], BF16)
        gsum = salloc("q_gsum", [128, 128], F32)
        gscl = salloc("q_gscl", [128, 128], BF16)
        grs = salloc("q_grs", [128, 128], F32)
        ps_t = [palloc(f"q_pst{i}", [128, 128], BF16) for i in range(2)]

        qstate = {"i": 0, "wt_war": [None, None], "xq_war": None, "st": None,
                  "ps_war": [None, None]}

        def quant_tile(src_rows_ap, wide, ndc, dst_col_ap):
            i = qstate["i"]
            qstate["i"] += 1
            buf = wt[i % 2]
            ng = wide // 64
            t_ld = p.dma("gpsimd", buf[:, 0:wide], src_rows_ap,
                         wait=(qstate["wt_war"][i % 2],), inc=f"t{i % 2}")
            b3 = buf[:, 0:wide].rearrange("p (g k) -> p g k", k=64)
            t1 = p.op("vector", lambda e, b3=b3, ng=ng: e.reduce_sum(
                gsum[:, 0:ng], b3, axis=AX.X, apply_absolute_value=True),
                wait=(t_ld,), inc="pv")
            t2 = p.op("vector", lambda e, ng=ng: e.tensor_scalar(
                out=gscl[:, 0:ng], in0=gsum[:, 0:ng],
                scalar1=1.0 / 64.0, scalar2=1e-8, op0=ALU.mult, op1=ALU.max),
                wait=(t1,), inc="pv")
            t3 = p.op("vector", lambda e, ng=ng: e.reciprocal(
                grs[:, 0:ng], gscl[:, 0:ng]), wait=(t2,), inc="pv")
            rs3 = grs[:, 0:ng].unsqueeze(2).broadcast_to([128, ng, 64])
            x3 = xq[:, 0:wide].rearrange("p (g k) -> p g k", k=64)
            t4 = p.op("vector", lambda e, b3=b3, rs3=rs3, x3=x3: e.tensor_tensor(
                x3, b3, rs3, ALU.mult), wait=(t3, qstate["xq_war"]), inc="pv")
            qstate["wt_war"][i % 2] = t4
            p.op("vector", lambda e, wide=wide: e.tensor_single_scalar(
                abuf[:, 0:wide], xq[:, 0:wide], 0.5, ALU.is_gt), inc="pv")
            p.op("vector", lambda e, wide=wide: e.tensor_single_scalar(
                bbuf[:, 0:wide], xq[:, 0:wide], -0.5, ALU.is_lt), inc="pv")
            p.op("vector", lambda e, wide=wide: e.tensor_tensor(
                abuf[:, 0:wide], abuf[:, 0:wide], bbuf[:, 0:wide], ALU.subtract),
                inc="pv")
            sc3 = gscl[:, 0:ng].unsqueeze(2).broadcast_to([128, ng, 64])
            a3 = abuf[:, 0:wide].rearrange("p (g k) -> p g k", k=64)
            t8 = p.op("vector", lambda e, x3=x3, a3=a3, sc3=sc3: e.tensor_tensor(
                x3, a3, sc3, ALU.mult), inc="pv")
            ev = None
            ttr = None
            evs = []
            for dc in range(ndc):
                wtr = [t8]
                if dc >= 2:
                    wtr.append(evs[dc - 2])
                else:
                    wtr.append(qstate["ps_war"][dc % 2])
                ttr = p.op("tensor", lambda e, dc=dc: e.transpose(
                    ps_t[dc % 2][:], xq[:, dc * 128:(dc + 1) * 128], id_b[:]),
                    wait=tuple(wtr), inc="pt")
                ev = p.op("scalar", lambda e, dc=dc: e.activation(
                    slabT[:, dc * 128:(dc + 1) * 128], ps_t[dc % 2][:], AF.Copy),
                    wait=(ttr, qstate["st"]), inc="pa")
                evs.append(ev)
            qstate["xq_war"] = ttr
            qstate["ps_war"] = [evs[-2], evs[-1]]
            st = None
            grp = ndc // 4
            for piece in range(4):
                st = p.dma(
                    "sync",
                    dst_col_ap[:, piece * grp:(piece + 1) * grp, :],
                    slabT[:, piece * grp * 128:(piece + 1) * grp * 128]
                    .rearrange("p (dc hh) -> p dc hh", hh=128),
                    wait=(ev,) if piece == 0 else (), inc="s0")
            qstate["st"] = st
            return st

        for i in range(16):
            dst = wguT_loc.ap()[:, i * 128:(i + 1) * 128].rearrange(
                "(dc pp) hh -> pp dc hh", pp=128)
            last_store = quant_tile(wgu_d[i * 128:(i + 1) * 128, :], DM, NDC, dst)
        for i in range(2):
            dst = wpT_loc.ap()[:, i * 128:(i + 1) * 128].rearrange(
                "(dc pp) hh -> pp dc hh", pp=128)
            last_store = quant_tile(wp_d[i * 128:(i + 1) * 128, :], HF, NHC, dst)

        Wc_bf = salloc("Wc_bf", [128, NDC * RANK], BF16)
        Wup_bf = salloc("Wup_bf", [128, 4 * RANK], BF16)
        for dc in range(NDC):
            p.dma("gpsimd", Wc_bf[:, dc * RANK:(dc + 1) * RANK],
                  Wc_d[dc * 128:(dc + 1) * 128, :], inc="wc")
        for rc in range(4):
            p.dma("gpsimd", Wup_bf[:, rc * RANK:(rc + 1) * RANK],
                  Wup_d[rc * 128:(rc + 1) * 128, :], inc="wc")
        t_wck = ("wc", p.cnt["wc"])

        p.barrier()
        p.mark("Q")
        sfree("q_wt0", "q_wt1", "q_xq", "q_a", "q_b", "q_slab",
              "q_gsum", "q_gscl", "q_grs", "q_pst0", "q_pst1")

        sq_scr = salloc("sq_scr", [128, DM], F32)
        ln_war = {"tok": None}

        def ln_stats_chunk(xc_ap, col, tok_ld, scr):
            t1 = p.op("vector", lambda e, xc_ap=xc_ap, col=col: e.reduce_sum(
                sm[:, col:col + 1], xc_ap, axis=AX.X),
                wait=(tok_ld,), inc="pv")
            t2 = p.op("scalar", lambda e, xc_ap=xc_ap, col=col, scr=scr:
                      e.activation(
                scr[:], xc_ap, AF.Square, accum_out=sm[:, 32 + col:33 + col]),
                wait=(tok_ld, ln_war["tok"]), inc="pa")
            return t1, t2

        def ln_finish(nch, toks):
            w = tuple(toks)
            c0, c1, c2, c3 = 0, 32, 64, 96
            p.op("vector", lambda e: e.tensor_scalar(
                out=sm[:, c0:c0 + nch], in0=sm[:, c0:c0 + nch],
                scalar1=1.0 / DM, scalar2=None, op0=ALU.mult), wait=w, inc="pv")
            tsn = p.op("vector", lambda e: e.tensor_scalar(
                out=sm[:, c1:c1 + nch], in0=sm[:, c1:c1 + nch],
                scalar1=1.0 / DM, scalar2=None, op0=ALU.mult), inc="pv")
            ln_war["tok"] = tsn
            p.op("vector", lambda e: e.tensor_tensor(
                sm[:, c3:c3 + nch], sm[:, c0:c0 + nch], sm[:, c0:c0 + nch],
                ALU.mult), inc="pv")
            p.op("vector", lambda e: e.tensor_tensor(
                sm[:, c1:c1 + nch], sm[:, c1:c1 + nch], sm[:, c3:c3 + nch],
                ALU.subtract), inc="pv")
            tv = p.op("vector", lambda e: e.tensor_scalar(
                out=sm[:, c1:c1 + nch], in0=sm[:, c1:c1 + nch],
                scalar1=EPS_LN, scalar2=None, op0=ALU.add), inc="pv")
            ts_ = p.op("scalar", lambda e: e.activation(
                sm[:, c3:c3 + nch], sm[:, c1:c1 + nch], AF.Sqrt),
                wait=(tv,), inc="pa")
            tr = p.op("vector", lambda e: e.reciprocal(
                sm[:, c2:c2 + nch], sm[:, c3:c3 + nch]), wait=(ts_,), inc="pv")
            p.op("vector", lambda e: e.tensor_tensor(
                sm[:, c3:c3 + nch], sm[:, c2:c2 + nch], sm[:, c2:c2 + nch],
                ALU.mult), wait=(tr,), inc="pv")
            p.op("vector", lambda e: e.tensor_tensor(
                sm[:, c3:c3 + nch], sm[:, c3:c3 + nch], sm[:, c1:c1 + nch],
                ALU.mult), inc="pv")
            p.op("vector", lambda e: e.tensor_scalar(
                out=sm[:, c3:c3 + nch], in0=sm[:, c3:c3 + nch],
                scalar1=-0.5, scalar2=1.5, op0=ALU.mult, op1=ALU.add), inc="pv")
            return p.op("vector", lambda e: e.tensor_tensor(
                sm[:, c2:c2 + nch], sm[:, c3:c3 + nch], sm[:, c2:c2 + nch],
                ALU.mult), inc="pv")

        def ln_transpose_chunk(nb, dst, dst_col_stride, tc, ta, war_ring,
                               dst2=None, tc2=0):
            evs = []
            for dc in range(NDC):
                wtr = [ta]
                if dc >= 2:
                    wtr.append(evs[dc - 2])
                elif war_ring[dc % 2] is not None:
                    wtr.append(war_ring[dc % 2])
                ttr = p.op("tensor", lambda e, nb=nb, dc=dc: e.transpose(
                    ps_tr[dc % 2][:], nb[:, dc * 128:(dc + 1) * 128], id_b[:]),
                    wait=tuple(wtr), inc="pt")
                d = dst[:, dc * dst_col_stride + tc * 128:
                        dc * dst_col_stride + tc * 128 + 128]
                if dc % 2 == 0:
                    ev = p.op("vector", lambda e, dc=dc, d=d: e.tensor_copy(
                        d, ps_tr[dc % 2][:]), wait=(ttr,), inc="pv")
                else:
                    ev = p.op("scalar", lambda e, dc=dc, d=d: e.activation(
                        d, ps_tr[dc % 2][:], AF.Copy), wait=(ttr,), inc="pa")
                evs.append(ev)
            war_ring[0] = evs[-2]
            war_ring[1] = evs[-1]
            return evs[-1]

        xn1T_own = salloc("xn1T_own", [128, NDC * TOK], BF16)
        K_loc = salloc("K_loc", [128, G * TOK], BF16)
        VT_loc = salloc("VT_loc", [128, 8 * RANK], BF16)
        xc = [salloc(f"a_xc{i}", [128, DM], F32) for i in range(4)]
        xnb = [salloc(f"a_xnb{i}", [128, DM], BF16) for i in range(2)]
        kvc = salloc("a_kvc", [128, 4 * 512], BF16)
        kmax2 = salloc("kmax2", [1, 8], F32, side="right")
        klb = salloc("klb", [1, 4], BF16, side="right")
        ps_tr = [palloc(f"ps_tr{i}", [128, 128], BF16) for i in range(2)]
        ps_mm = [palloc(f"ps_mm{i}", [128, 512], F32) for i in range(2)]

        xc_war = [None] * 4
        xnb_war = [None, None]
        tr_war = [None, None]
        mm_war = [None, None]
        prev_ta = None
        last_k_ev = None
        for j in range(2):
            ch = []
            for c in range(4):
                cidx = j * 4 + c
                buf = xc[c]
                t_ld = p.dma("scalar", buf[:],
                             x_own_d[cidx * 128:(cidx + 1) * 128, :],
                             wait=(xc_war[c],), inc=f"a{c}")
                t1, t2 = ln_stats_chunk(buf[:], c, t_ld, sq_scr)
                ch.append((buf, c, t1, t2))
            t_stats = ln_finish(4, [t for (_, _, t1, t2) in ch
                                    for t in (t1, t2)] + [prev_ta])
            last_ev = None
            for (buf, c, _, _) in ch:
                nb = xnb[c % 2]
                wpe = [t_stats, xnb_war[c % 2]]
                ta = p.op("vector", lambda e, buf=buf, nb=nb, c=c: e.tensor_scalar(
                    out=nb[:], in0=buf[:],
                    scalar1=sm[:, c:c + 1], scalar2=sm[:, 64 + c:65 + c],
                    op0=ALU.subtract, op1=ALU.mult), wait=tuple(wpe), inc="pv")
                xc_war[c] = ta
                last_ev = ln_transpose_chunk(
                    nb, xn1T_own, TOK, j * 4 + c, ta, tr_war)
                xnb_war[c % 2] = last_ev
                prev_ta = ta
            if j == 1:
                t_xn1own = last_ev
            kv_evs = []
            for rc in range(4):
                mm = None
                for dc in range(NDC):
                    w = (last_ev, t_wck, mm_war[rc % 2]) if dc == 0 else ()
                    mm = p.op("tensor", lambda e, rc=rc, dc=dc, j=j: e.matmul(
                        ps_mm[rc % 2][:],
                        Wc_bf[:, dc * RANK + rc * 128: dc * RANK + rc * 128 + 128],
                        xn1T_own[:, dc * TOK + j * 512: dc * TOK + (j + 1) * 512],
                        start=(dc == 0), stop=(dc == NDC - 1)),
                        wait=w, inc=("pt" if dc == NDC - 1 else None))
                ev = p.op("vector", lambda e, rc=rc: e.tensor_scalar(
                    out=kvc[:, rc * 512:(rc + 1) * 512], in0=ps_mm[rc % 2][:],
                    scalar1=SIG1, scalar2=None, op0=ALU.mult),
                    wait=(mm,), inc="pv")
                kv_evs.append(ev)
                mm_war[rc % 2] = ev
            t_kvc = kv_evs[-1]
            for g in range(G):
                mm = None
                for rc in range(4):
                    w = (t_kvc, mm_war[g % 2]) if rc == 0 else ()
                    mm = p.op("tensor", lambda e, g=g, rc=rc: e.matmul(
                        ps_mm[g % 2][:],
                        Wup_bf[:, rc * RANK + g * 128: rc * RANK + g * 128 + 128],
                        kvc[:, rc * 512:(rc + 1) * 512],
                        start=(rc == 0), stop=(rc == 3)),
                        wait=w, inc=("pt" if rc == 3 else None))
                ev = p.op("vector", lambda e, g=g, j=j: e.tensor_copy(
                    K_loc[:, g * TOK + j * 512: g * TOK + (j + 1) * 512],
                    ps_mm[g % 2][:]), wait=(mm,), inc="pv")
                mm_war[g % 2] = ev
                last_k_ev = ev
            for tc in range(4):
                mm = None
                for rc in range(4):
                    w = (t_kvc, mm_war[tc % 2]) if rc == 0 else ()
                    mm = p.op("tensor", lambda e, tc=tc, rc=rc: e.matmul(
                        ps_mm[tc % 2][:],
                        kvc[:, rc * 512 + tc * 128: rc * 512 + tc * 128 + 128],
                        Wup_bf[:, rc * RANK:(rc + 1) * RANK],
                        start=(rc == 0), stop=(rc == 3)),
                        wait=w, inc=("pt" if rc == 3 else None))
                ev = p.op("vector", lambda e, tc=tc, j=j: e.tensor_copy(
                    VT_loc[:, (j * 4 + tc) * RANK:(j * 4 + tc + 1) * RANK],
                    ps_mm[tc % 2][:]), wait=(mm,), inc="pv")
                mm_war[tc % 2] = ev
        t_kvdone = ev

        ksq = salloc("ksq", [128, G * TOK], BF16)
        ps_row = palloc("ps_row", [1, 512], F32)
        ev = None
        tqk = p.op("scalar", lambda e: e.activation(
            ksq[:], K_loc[:], AF.Square), wait=(last_k_ev,), inc="pa")
        for g in range(G):
            for sl in range(2):
                mm = p.op("tensor", lambda e, g=g, sl=sl: e.matmul(
                    ps_row[:], ones_b[:, 0:1],
                    ksq[:, g * TOK + sl * 512: g * TOK + (sl + 1) * 512],
                    start=True, stop=True), wait=(tqk, ev), inc="pt")
                ev = p.op("vector", lambda e, sl=sl: e.tensor_copy(
                    rowbuf[0:1, sl * 512:(sl + 1) * 512], ps_row[:]),
                    wait=(mm,), inc="pv")
            tm = p.op("vector", lambda e, g=g: e.reduce_max(
                kmax2[0:1, 4 + g:5 + g], rowbuf[0:1, 0:TOK], axis=AX.X),
                wait=(ev,), inc="pv")
        t_klb = p.op("vector", lambda e: e.tensor_copy(
            klb[0:1, :], kmax2[0:1, 4:8]), wait=(tm,), inc="pv")
        q2s_seed = (ev, mm)

        t_kst = p.dma("sync", kv_gin.ap()[:, 0:G * TOK], K_loc[:],
                      wait=(last_k_ev,), inc="dq")
        t_vst = p.dma("sync", kv_gin.ap()[:, G * TOK:2 * G * TOK], VT_loc[:],
                      wait=(t_kvdone,), inc="dq")
        t_mst = p.dma("sync", kv_gin.ap()[0:1, 2 * G * TOK:2 * G * TOK + 4],
                      klb[0:1, :], wait=(t_klb,), inc="dq")

        p.barrier()
        p.mark("A")
        sfree("a_xc0", "a_xc1", "a_xc2", "a_xc3", "a_xnb0", "a_xnb1",
              "a_kvc", "Wc_bf", "Wup_bf", "K_loc", "VT_loc", "ksq",
              "sq_scr", "klb")

        K_sb = salloc("K_sb", [128, G * S], BF16)
        VT_sb = salloc("VT_sb", [128, NKC * RANK], BF16)
        q_bf = salloc("q_bf", [128, H * TOK], BF16)
        wqs = [salloc(f"wqs{i}", [128, NDC * 512], BF16) for i in range(3)]
        qsq = salloc("qsq", [128, TOK], BF16)
        qmax2 = salloc("qmax2", [1, 16], F32, side="right")
        km16 = salloc("km16", [1, 16], BF16, side="right")
        rowbuf2 = salloc("rowbuf2", [1, 2048], F32, side="right")
        ps_q = [palloc(f"ps_q{i}", [128, 512], F32) for i in range(2)]

        slab_toks = []
        slab_last_mm = [None] * 4
        for os_ in range(3):
            t_slab = None
            for dc in range(NDC):
                t_slab = p.dma(
                    "gpsimd", wqs[os_][:, dc * 512:(dc + 1) * 512],
                    Wq_d[dc * 128:(dc + 1) * 128, os_ * 512:(os_ + 1) * 512],
                    inc=f"q{os_}")
            slab_toks.append(t_slab)
        t_cck = p.op("gpsimd", lambda e: e.collective_compute(
            "AllGather", ALU.bypass,
            replica_groups=[list(range(NCORES))],
            ins=[kv_gin.ap().opt()], outs=[kv_gall.ap().opt()]),
            wait=(t_kst, t_vst, t_mst), inc="cc3", is_dma=True)
        km16b = salloc("km16b", [1, 16], BF16, side="right")
        slab_done = [None] * 4
        kvv = kv_gall.ap()
        for r in range(4):
            p.dma("sync",
                  K_sb[:, :].rearrange("p (g rr t) -> p g rr t", g=G, rr=4)
                  [:, :, r, :],
                  kvv[r * 128:(r + 1) * 128, 0:G * TOK]
                  .rearrange("p (g t) -> p g t", t=TOK),
                  wait=(t_cck,) if r == 0 else (), inc="aq")
        for r in range(4):
            p.dma("sync", VT_sb[:, r * 8 * RANK:(r + 1) * 8 * RANK],
                  kvv[r * 128:(r + 1) * 128, G * TOK:2 * G * TOK], inc="aq")
        p.dma("sync",
              km16[0:1, :].rearrange("p (r c) -> p r c", c=4),
              kvv.rearrange("(r pp) c -> pp r c", pp=128)
              [0:1, 0:4, 2 * G * TOK:2 * G * TOK + 4],
              inc="aq")
        p.dma("sync",
              km16b[0:1, :].rearrange("p (r c) -> p r c", c=4),
              kvv.rearrange("(r pp) c -> pp r c", pp=128)
              [0:1, 4:8, 2 * G * TOK:2 * G * TOK + 4],
              inc="aq")
        t_Aall = ("aq", p.cnt["aq"])

        slab3_dma_slot = len(p.ops)

        if "kvgall" in DEBUG_DUMPS:
            dkv = nc.dram_tensor("dbg_kvgall", [NCORES * 128, 2 * G * TOK],
                                 BF16, kind="ExternalOutput").ap()
            p.dma("scalar", dkv, kvv[:, 0:2 * G * TOK],
                  wait=(t_cck,), inc="sq")
        if "kvgin" in DEBUG_DUMPS:
            dki = nc.dram_tensor("dbg_kvgin", [128, 2 * G * TOK],
                                 BF16, kind="ExternalOutput").ap()
            p.dma("scalar", dki, kv_gin.ap()[:, 0:2 * G * TOK],
                  wait=(t_kst, t_vst, t_mst), inc="sq")

        def stage_slot(i):
            buf = wqs[1 + (i % 4) // 2]
            off = (i % 2) * (G * TOK)
            return buf[:, off:off + G * TOK], slab_done[1 + (i % 4) // 2]

        def emit_kv_blend():
            t_m0k = p.op("vector", lambda e: e.tensor_scalar(
                out=K_sb[:], in0=K_sb[:], scalar1=msk[:, 0:1], scalar2=None,
                op0=ALU.mult), wait=(t_Aall,), inc="pv")
            t_m0v = p.op("vector", lambda e: e.tensor_scalar(
                out=VT_sb[:], in0=VT_sb[:], scalar1=msk[:, 0:1], scalar2=None,
                op0=ALU.mult), inc="pv")
            war = [None] * 4
            wave_tok = [None, None]
            for wv in range(2):
                for i in range(wv * 4, wv * 4 + 4):
                    r = i % 4
                    slot, t_free = stage_slot(i)
                    src = kvv[(4 + r) * 128:(5 + r) * 128,
                              (0 if wv == 0 else G * TOK):
                              (G * TOK if wv == 0 else 2 * G * TOK)]
                    wave_tok[wv] = p.dma("sync", slot, src,
                                         wait=(war[i % 4], t_free), inc="dq")
                if wv == 0:
                    for i in range(4):
                        r = i % 4
                        slot, _ = stage_slot(i)
                        dst = K_sb[:, :].rearrange(
                            "p (g rr t) -> p g rr t", g=G, rr=4)[:, :, r, :]
                        sview = slot.rearrange("p (g t) -> p g t", t=TOK)
                        war[i] = p.op(
                            "vector",
                            lambda e, dst=dst, sview=sview: e.scalar_tensor_tensor(
                                out=dst, in0=sview, scalar=msk[:, 1:2], in1=dst,
                                op0=ALU.mult, op1=ALU.add),
                            wait=(wave_tok[0], t_m0k), inc="pv")
            last = None
            for i in range(4):
                slot, _ = stage_slot(i + 4)
                dst = VT_sb[:, i * 8 * RANK:(i + 1) * 8 * RANK]
                last = p.op(
                    "vector",
                    lambda e, dst=dst, slot=slot: e.scalar_tensor_tensor(
                        out=dst, in0=slot, scalar=msk[:, 1:2], in1=dst,
                        op0=ALU.mult, op1=ALU.add),
                    wait=(wave_tok[1], t_m0v), inc="pv")
            p.op("vector", lambda e: e.tensor_scalar(
                out=km16[0:1, :], in0=km16[0:1, :], scalar1=msk[0:1, 0:1],
                scalar2=None, op0=ALU.mult), inc="pv")
            return p.op("vector", lambda e: e.scalar_tensor_tensor(
                out=km16[0:1, :], in0=km16b[0:1, :], scalar=msk[0:1, 1:2],
                in1=km16[0:1, :], op0=ALU.mult, op1=ALU.add),
                inc="pv")

        psq_war = [None, None]
        q2s = {"ev2": q2s_seed[0], "qsq_war": q2s_seed[1], "tmx": None,
               "pending": None}

        def flush_q2():
            if q2s["pending"] is not None:
                q2s["pending"]()
                q2s["pending"] = None

        def make_q2(h, ev):
            def emit():
                tq = p.op("scalar", lambda e, h=h: e.activation(
                    qsq[:], q_bf[:, h * TOK:(h + 1) * TOK], AF.Square),
                    wait=(ev, q2s["qsq_war"]), inc="pa")
                for sl in range(NQT):
                    mm2 = p.op("tensor", lambda e, sl=sl: e.matmul(
                        ps_row[:], ones_b[:, 0:1], qsq[:, sl * 512:(sl + 1) * 512],
                        start=True, stop=True), wait=(tq, q2s["ev2"]), inc="pt")
                    q2s["ev2"] = p.op("vector", lambda e, sl=sl: e.tensor_copy(
                        rowbuf2[0:1, sl * 512:(sl + 1) * 512], ps_row[:]),
                        wait=(mm2,), inc="pv")
                    q2s["qsq_war"] = mm2
                q2s["tmx"] = p.op("vector", lambda e, h=h: e.reduce_max(
                    qmax2[0:1, h:h + 1], rowbuf2[0:1, 0:TOK], axis=AX.X),
                    wait=(q2s["ev2"],), inc="pv")
            return emit

        for os_ in range(4):
            t_slab = slab_toks[os_]
            for oc4 in range(4):
                h = os_ * 4 + oc4
                ev = None
                for qt in range(NQT):
                    u = (oc4 * NQT + qt) % 2
                    mm = None
                    for dc in range(NDC):
                        w = (t_slab, t_xn1own, psq_war[u]) if dc == 0 else ()
                        mm = p.op("tensor",
                                  lambda e, os_=os_, oc4=oc4, qt=qt, dc=dc, u=u:
                                  e.matmul(
                            ps_q[u][:],
                            wqs[os_ % 3][:, dc * 512 + oc4 * 128:
                                         dc * 512 + oc4 * 128 + 128],
                            xn1T_own[:, dc * TOK + qt * 512:
                                     dc * TOK + (qt + 1) * 512],
                            start=(dc == 0), stop=(dc == NDC - 1)),
                            wait=w, inc=("pt" if dc == NDC - 1 else None))
                    ev = p.op("vector", lambda e, h=h, qt=qt, u=u: e.tensor_copy(
                        q_bf[:, h * TOK + qt * 512: h * TOK + (qt + 1) * 512],
                        ps_q[u][:]), wait=(mm,), inc="pv")
                    psq_war[u] = ev
                    if qt == 0:
                        flush_q2()
                q2s["pending"] = make_q2(h, ev)
            slab_done[os_] = mm
            if os_ == 0:
                war = mm
                for dc in range(NDC):
                    fn = (lambda e, dc=dc: e.dma_start(
                        out=wqs[0][:, dc * 512:(dc + 1) * 512],
                        in_=Wq_d[dc * 128:(dc + 1) * 128, 1536:2048]))
                    p.cnt["q0"] += 16
                    tok3 = ("q0", p.cnt["q0"])
                    p.ops.insert(slab3_dma_slot + dc,
                                 ("gpsimd", fn,
                                  (war,) if dc == 0 else (), "q0", 16))
                slab_toks.append(tok3)
        flush_q2()
        t_qmax = q2s["tmx"]
        t_kvbl = emit_kv_blend()
        t_cc1 = p.op("gpsimd", lambda e: e.collective_compute(
            "AllGather", ALU.bypass, replica_groups=[list(range(NCORES))],
            ins=[wguT_loc.ap().opt()], outs=[wguT_all.ap().opt()]),
            wait=(last_store, t_kvbl), inc="cc1", is_dma=True)
        t_cc2 = p.op("gpsimd", lambda e: e.collective_compute(
            "AllGather", ALU.bypass, replica_groups=[list(range(NCORES))],
            ins=[wpT_loc.ap().opt()], outs=[wpT_all.ap().opt()]),
            inc="cc2", is_dma=True)
        t_kmax = p.op("vector", lambda e: e.reduce_max(
            kmax2[0:1, 0:4],
            km16[0:1, :].rearrange("p (r g) -> p g r", g=4), axis=AX.X),
            wait=(t_kvbl,), inc="pv")
        p.barrier()
        p.mark("B1")
        dbg("K", K_sb, [128, G * S], BF16)
        dbg("VT", VT_sb, [128, NKC * RANK], BF16)
        dbg("q", q_bf, [128, H * TOK], BF16)
        sfree("wqs0", "wqs1", "wqs2", "wqs3", "qsq", "xn1T_own",
              "ps_q0", "ps_q1")

        negc0_bf = salloc("negc0_bf", [1, 16], BF16, side="right")
        negc0T = salloc("negc0T", [128, 16], F32, side="right")
        ps_c0 = palloc("ps_c0", [128, 16], F32)
        c0row = rowbuf[0:1, 0:16]
        km3 = kmax2[0:1, 0:4].unsqueeze(2).broadcast_to([1, 4, 4])
        qm3 = qmax2[0:1, :].rearrange("p (a b) -> p a b", b=4)
        c3v = c0row.rearrange("p (a b) -> p a b", b=4)
        t1 = p.op("vector", lambda e: e.tensor_tensor(
            c3v, qm3, km3, ALU.add), wait=(t_kmax, t_qmax), inc="pv")
        t2 = p.op("vector", lambda e: e.tensor_scalar(
            out=negc0_bf[0:1, :], in0=c0row,
            scalar1=-0.5 * SCALE, scalar2=None, op0=ALU.mult),
            wait=(t1,), inc="pv")
        t3 = p.op("tensor", lambda e: e.matmul(
            ps_c0[:], ones_b[0:1, :], negc0_bf[0:1, :], start=True, stop=True),
            wait=(t2,), inc="pt")
        t_negc0 = p.op("vector", lambda e: e.tensor_copy(
            negc0T[:], ps_c0[:]), wait=(t3,), inc="pv")
        p.barrier()
        p.mark("B2")
        dbg("kmax2", kmax2, [1, 8], F32)
        dbg("qmax2", qmax2, [1, 16], F32)
        dbg("negc0T", negc0T, [128, 16], F32)
        sfree("ps_c0", "ps_tr0", "ps_tr1", "ps_mm0", "ps_mm1", "ps_row",
              "rowbuf", "km16", "km16b", "kmax2", "qmax2")

        attn_fm = salloc("attn_fm", [128, H * TOK], BF16)
        PT = salloc("PT", [128, 4 * 512], BF16)
        ibc = [salloc(f"ibc{i}", [128, 512], F32) for i in range(2)]
        ps_s = [palloc(f"ps_s{i}", [128, 1024], F32) for i in range(2)]
        ps_pv = [palloc(f"ps_pv{i}", [128, 512], F32) for i in range(2)]
        ps_sb = [palloc(f"ps_sb{i}", [128, 512], F32) for i in range(2)]

        NG = NKC // 2
        unit_done = [t_negc0, t_negc0]
        last_pv = [None, None]
        uidx = 0
        pend_pe = [None]

        def flush_pe():
            if pend_pe[0] is not None:
                pend_pe[0]()
                pend_pe[0] = None

        for h in range(H):
            g = h // 4
            for qt in range(NQT):
                pi = uidx % 2
                tok_sc = [None] * NG
                tok_exp = [None] * NG
                tok_pv = [None] * NG

                def emit_scores(i):
                    tok = None
                    for half in range(2):
                        kc = i * 2 + half
                        tok = p.op("tensor",
                                   lambda e, i=i, half=half, kc=kc, h=h, qt=qt, g=g:
                                   e.matmul(
                            ps_s[i % 2][:, half * 512:(half + 1) * 512],
                            K_sb[:, g * S + kc * 128: g * S + kc * 128 + 128],
                            q_bf[:, h * TOK + qt * 512: h * TOK + (qt + 1) * 512],
                            start=True, stop=True, skip_group_check=True),
                            inc=("pt" if half == 1 else None))
                    tok_sc[i] = tok

                def emit_exp(i):
                    w = [tok_sc[i], last_pv[i % 2]]
                    tok_exp[i] = p.op("scalar", lambda e, i=i, h=h: e.activation(
                        PT[:, (i % 2) * 1024:(i % 2) * 1024 + 1024],
                        ps_s[i % 2][:], AF.Exp,
                        bias=negc0T[:, h:h + 1], scale=SCALE),
                        wait=tuple(w), inc="pa")

                def emit_pv(i):
                    tok = None
                    for half in range(2):
                        kc = i * 2 + half
                        slot = PT[:, (i % 2) * 1024 + half * 512:
                                  (i % 2) * 1024 + (half + 1) * 512]
                        w = [tok_exp[i]] if half == 0 else []
                        if i == 0 and half == 0:
                            w.append(unit_done[pi])
                        p.op("tensor", lambda e, kc=kc, g=g, slot=slot, pi=pi:
                             e.matmul(
                            ps_pv[pi][:],
                            VT_sb[:, kc * RANK + g * 128: kc * RANK + g * 128 + 128],
                            slot, start=(kc == 0), stop=(kc == NKC - 1),
                            skip_group_check=True), wait=tuple(w))
                        tok = p.op("tensor", lambda e, kc=kc, slot=slot, pi=pi:
                                   e.matmul(
                            ps_sb[pi][0:1, :], ones_b[:, 0:1], slot,
                            start=(kc == 0), stop=(kc == NKC - 1),
                            skip_group_check=True),
                            inc=("pt" if half == 1 else None))
                    tok_pv[i] = tok
                    last_pv[i % 2] = tok

                emit_scores(0)
                for i in range(1, NG):
                    emit_scores(i)
                    emit_exp(i - 1)
                    emit_pv(i - 1)
                    if i == 3:
                        flush_pe()
                emit_exp(NG - 1)
                emit_pv(NG - 1)

                t_se = p.op("vector", lambda e, pi=pi: e.tensor_copy(
                    rowbuf2[0:1, pi * 1024:pi * 1024 + 512], ps_sb[pi][0:1, :]),
                    wait=(tok_pv[NG - 1],), inc="pv")
                t_ri = p.op("vector", lambda e, pi=pi: e.reciprocal(
                    rowbuf2[0:1, pi * 1024 + 512:pi * 1024 + 1024],
                    rowbuf2[0:1, pi * 1024:pi * 1024 + 512]),
                    wait=(t_se,), inc="pv")

                def mk_tail(h=h, qt=qt, pi=pi, t_ri=t_ri):
                    def emit():
                        t_bc = p.op("tensor", lambda e, pi=pi: e.matmul(
                            ps_sb[pi][:], ones_f[0:1, :],
                            rowbuf2[0:1, pi * 1024 + 512:pi * 1024 + 1024],
                            start=True, stop=True, skip_group_check=True),
                            wait=(t_ri,), inc="pt")
                        t_ie = p.op("vector", lambda e, pi=pi: e.tensor_copy(
                            ibc[pi][:], ps_sb[pi][:]), wait=(t_bc,), inc="pv")
                        unit_done[pi] = p.op(
                            "vector", lambda e, h=h, qt=qt, pi=pi: e.tensor_tensor(
                                attn_fm[:, h * TOK + qt * 512:
                                        h * TOK + (qt + 1) * 512],
                                ps_pv[pi][:], ibc[pi][:], ALU.mult),
                            wait=(t_ie,), inc="pv")
                    return emit
                pend_pe[0] = mk_tail()
                uidx += 1
        flush_pe()
        tok_unit_done = unit_done[(uidx - 1) % 2]

        p.barrier()
        p.mark("B3")
        dbg("attn", attn_fm, [128, H * TOK], BF16)
        sfree("q_bf", "K_sb", "VT_sb", "PT", "ibc0", "ibc1", "negc0_bf",
              "negc0T", "rowbuf2", "ps_s0", "ps_s1", "ps_pv0", "ps_pv1",
              "ps_sb0", "ps_sb1")

        xo2 = salloc("xo2", [128, NTC * DM], F32)
        wos = [salloc(f"wos{i}", [128, H * 512], BF16) for i in range(2)]
        ps_o = [palloc(f"ps_o{i}", [128, 512], F32) for i in range(2)]

        for tc in range(NTC):
            p.dma("scalar", xo2[:, tc * DM:(tc + 1) * DM],
                  x_own_d[tc * 128:(tc + 1) * 128, :], inc="aq")
        t_ld2 = ("aq", p.cnt["aq"])
        ld2 = [t_ld2] * NTC
        wos_war = [None, None]
        pso_war = [None, None]
        for os_ in range(4):
            t_slab = None
            for cc in range(H):
                t_slab = p.dma(
                    "gpsimd", wos[os_ % 2][:, cc * 512:(cc + 1) * 512],
                    Wo_d[cc * 128:(cc + 1) * 128, os_ * 512:(os_ + 1) * 512],
                    wait=((wos_war[os_ % 2],) if cc == 0 else ()),
                    inc=f"o{os_ % 2}")
            last_mm = None
            for tc in range(NTC):
                mm = None
                for cc in range(H):
                    w = (t_slab, pso_war[tc % 2]) if cc == 0 else ()
                    mm = p.op("tensor", lambda e, os_=os_, tc=tc, cc=cc: e.matmul(
                        ps_o[tc % 2][:],
                        attn_fm[:, cc * TOK + tc * 128: cc * TOK + tc * 128 + 128],
                        wos[os_ % 2][:, cc * 512:(cc + 1) * 512],
                        start=(cc == 0), stop=(cc == H - 1)),
                        wait=w, inc=("pt" if cc == H - 1 else None))
                ev = p.op("vector", lambda e, os_=os_, tc=tc: e.tensor_tensor(
                    xo2[:, tc * DM + os_ * 512: tc * DM + (os_ + 1) * 512],
                    ps_o[tc % 2][:],
                    xo2[:, tc * DM + os_ * 512: tc * DM + (os_ + 1) * 512],
                    ALU.add), wait=(mm, ld2[tc]), inc="pv")
                pso_war[tc % 2] = ev
                last_mm = mm
            wos_war[os_ % 2] = last_mm
        p.barrier()
        p.mark("B4")
        dbg("x1", xo2, [128, NTC * DM], F32)
        sfree("attn_fm", "wos0", "wos1", "ps_o0", "ps_o1")

        xn2T = salloc("xn2T", [128, NDC * TOK], BF16)
        xnb3 = [salloc(f"c_xnb{i}", [128, DM], BF16) for i in range(2)]
        sq_scr2 = salloc("sq_scr2", [128, DM], F32)
        ps_tr = [palloc(f"ps_tr{i}", [128, 128], BF16) for i in range(2)]
        sts = []
        for tc in range(NTC):
            t1, t2 = ln_stats_chunk(xo2[:, tc * DM:(tc + 1) * DM], tc, None,
                                    sq_scr2)
            sts += [t1, t2]
        t_stats = ln_finish(NTC, sts)
        xnb3_war = [None, None]
        tr_war = [None, None]
        t_xn2 = None
        for tc in range(NTC):
            nb = xnb3[tc % 2]
            ta = p.op("vector", lambda e, nb=nb, tc=tc: e.tensor_scalar(
                out=nb[:], in0=xo2[:, tc * DM:(tc + 1) * DM],
                scalar1=sm[:, tc:tc + 1], scalar2=sm[:, 64 + tc:65 + tc],
                op0=ALU.subtract, op1=ALU.mult),
                wait=(t_stats, xnb3_war[tc % 2]), inc="pv")
            t_xn2 = ln_transpose_chunk(nb, xn2T, TOK, tc, ta, tr_war)
            xnb3_war[tc % 2] = t_xn2
        p.barrier()
        p.mark("C0")
        dbg("xn2T", xn2T, [128, NDC * TOK], BF16)
        sfree("c_xnb0", "c_xnb1", "ps_tr0", "ps_tr1", "sq_scr2")

        h_sb = salloc("h_sb", [128, NHC * 512], BF16)
        QW = 256
        NQUAD = HF // QW
        wgt = [salloc(f"wgt{i}", [128, NDC * QW], BF16) for i in range(4)]
        silu_t = [salloc("silu_t0", [128, 512], BF16, side="right")] * 2
        h2_t = [salloc(f"h2_t{i}", [128, 512], BF16, side="right") for i in range(2)]
        wpt = [salloc(f"wpt{i}", [128, 1024], BF16, side="right") for i in range(3)]
        ssqT = salloc("ssqT", [128, 8], F32, side="right")
        r2T = salloc("r2T", [128, 8], F32, side="right")
        rmsrow = salloc("rmsrow", [1, 512], F32, side="right")

        wguT_all_ap = wguT_all.ap()
        wpT_all_v = wpT_all.ap().rearrange("(r hh) o -> hh r o", r=NCORES)

        def gu_tile_ap(quad, is_up):
            hglob = (HF if is_up else 0) + quad * QW
            r = hglob // GSH
            hl = hglob % GSH
            base = wguT_all_ap[r * DM:(r + 1) * DM, hl:hl + QW]
            return base.rearrange("(dc pp) h4 -> pp dc h4", pp=128)

        wgt_war = [None, None]
        ps_c2_war = [None] * 8
        for th in range(2):
            ps_g = [palloc(f"ps_g{i}", [128, 512], F32) for i in range(2)]
            ps_u = [palloc(f"ps_u{i}", [128, 512], F32) for i in range(2)]
            ps_rms = palloc("ps_rms", [1, 512], F32)
            ps_trf = palloc("ps_trf", [128, 4], F32)
            h_done = [None] * NHC
            rms_mms = [None] * NHC
            t2hs = [None] * NHC
            rms_mm = None
            for quad in range(NQUAD):
                w = [wgt_war[quad % 2]]
                if quad == 0:
                    w += [t_cc1, t_xn2]
                tg = tu = None
                for piece in range(4):
                    ds = slice(piece * 4 * QW, (piece + 1) * 4 * QW)
                    ss = slice(piece * 4, (piece + 1) * 4)
                    tg = p.dma("sync", wgt[(quad % 2) * 2][:, ds].rearrange(
                        "p (dc h4) -> p dc h4", h4=QW),
                        gu_tile_ap(quad, False)[:, ss, :],
                        wait=tuple(w) if piece == 0 else (),
                        inc=f"g{quad % 2}")
                for piece in range(4):
                    ds = slice(piece * 4 * QW, (piece + 1) * 4 * QW)
                    ss = slice(piece * 4, (piece + 1) * 4)
                    tu = p.dma("sync", wgt[(quad % 2) * 2 + 1][:, ds].rearrange(
                        "p (dc h4) -> p dc h4", h4=QW),
                        gu_tile_ap(quad, True)[:, ss, :],
                        inc=f"g{quad % 2}")
                for k in range(QW // 128):
                    pr = quad * (QW // 128) + k
                    pg, pu = ps_g[pr % 2], ps_u[pr % 2]
                    mmg = mmu = None
                    for dc in range(NDC):
                        wmm = (tg, tu, h_done[pr - 2] if pr >= 2 else None
                               ) if dc == 0 else ()
                        mmg = p.op("tensor",
                                   lambda e, quad=quad, k=k, dc=dc, th=th, pg=pg:
                                   e.matmul(
                            pg[:],
                            wgt[(quad % 2) * 2][:, dc * QW + k * 128:
                                                dc * QW + k * 128 + 128],
                            xn2T[:, dc * TOK + th * 512: dc * TOK + (th + 1) * 512],
                            start=(dc == 0), stop=(dc == NDC - 1),
                            skip_group_check=True),
                            wait=wmm, inc=("pt" if dc == NDC - 1 else None))
                    for dc in range(NDC):
                        mmu = p.op("tensor",
                                   lambda e, quad=quad, k=k, dc=dc, th=th, pu=pu:
                                   e.matmul(
                            pu[:],
                            wgt[(quad % 2) * 2 + 1][:, dc * QW + k * 128:
                                                    dc * QW + k * 128 + 128],
                            xn2T[:, dc * TOK + th * 512: dc * TOK + (th + 1) * 512],
                            start=(dc == 0), stop=(dc == NDC - 1),
                            skip_group_check=True),
                            inc=("pt" if dc == NDC - 1 else None))
                    tsl = p.op("scalar", lambda e, pg=pg, pr=pr: e.activation(
                        silu_t[pr % 2][:], pg[:], AF.Silu),
                        wait=(mmg, mmu, h_done[pr - 1] if pr >= 1 else None),
                        inc="pa")
                    if pr >= 1:
                        rms_mm = p.op("tensor", lambda e, pr=pr: e.matmul(
                            ps_rms[:], ones_b[:, 0:1], h2_t[(pr - 1) % 2][:],
                            start=(pr - 1 == 0), stop=False,
                            skip_group_check=True), wait=(t2hs[pr - 1],),
                            inc="pt")
                        rms_mms[pr - 1] = rms_mm
                    th_tok = p.op("vector", lambda e, pr=pr, pu=pu: e.tensor_tensor(
                        h_sb[:, pr * 512:(pr + 1) * 512], pu[:],
                        silu_t[pr % 2][:], ALU.mult), wait=(tsl,), inc="pv")
                    t2h = p.op("vector", lambda e, pr=pr: e.tensor_tensor(
                        h2_t[pr % 2][:], h_sb[:, pr * 512:(pr + 1) * 512],
                        h_sb[:, pr * 512:(pr + 1) * 512], ALU.mult),
                        wait=(th_tok, rms_mms[pr - 2] if pr >= 2 else None),
                        inc="pv")
                    t2hs[pr] = t2h
                    if pr == NHC - 1:
                        rms_mm = p.op("tensor", lambda e, pr=pr: e.matmul(
                            ps_rms[:], ones_b[:, 0:1], h2_t[pr % 2][:],
                            start=False, stop=True,
                            skip_group_check=True), wait=(t2h,), inc="pt")
                        rms_mms[pr] = rms_mm
                    h_done[pr] = th_tok
                wgt_war[quad % 2] = rms_mm
            t_re = p.op("vector", lambda e: e.tensor_copy(
                rmsrow[0:1, :], ps_rms[:]), wait=(rms_mm,), inc="pv")
            ttf = None
            for tc4 in range(4):
                ttf = p.op("tensor", lambda e, tc4=tc4: e.transpose(
                    ps_trf[:, tc4:tc4 + 1],
                    rmsrow[0:1, tc4 * 128:(tc4 + 1) * 128], id_f[0:1, 0:1]),
                    wait=(t_re,), inc="pt")
            cs = th * 4
            tev = p.op("vector", lambda e, cs=cs: e.tensor_copy(
                ssqT[:, cs:cs + 4], ps_trf[:]), wait=(ttf,), inc="pv")
            tss = p.op("vector", lambda e, cs=cs: e.tensor_scalar(
                out=ssqT[:, cs:cs + 4], in0=ssqT[:, cs:cs + 4],
                scalar1=1.0 / HF, scalar2=EPS_RMS, op0=ALU.mult, op1=ALU.add),
                wait=(tev,), inc="pv")
            tsq = p.op("scalar", lambda e, cs=cs: e.activation(
                r2T[:, cs:cs + 4], ssqT[:, cs:cs + 4], AF.Sqrt),
                wait=(tss,), inc="pa")
            trc = p.op("vector", lambda e, cs=cs: e.reciprocal(
                r2T[:, cs:cs + 4], r2T[:, cs:cs + 4]), wait=(tsq,), inc="pv")
            p.op("vector", lambda e, cs=cs: e.tensor_tensor(
                sm[:, 0:4], r2T[:, cs:cs + 4], r2T[:, cs:cs + 4], ALU.mult),
                wait=(trc,), inc="pv")
            p.op("vector", lambda e, cs=cs: e.tensor_tensor(
                sm[:, 0:4], sm[:, 0:4], ssqT[:, cs:cs + 4], ALU.mult), inc="pv")
            p.op("vector", lambda e: e.tensor_scalar(
                out=sm[:, 0:4], in0=sm[:, 0:4],
                scalar1=-0.5, scalar2=1.5, op0=ALU.mult, op1=ALU.add), inc="pv")
            t_r2 = p.op("vector", lambda e, cs=cs: e.tensor_tensor(
                r2T[:, cs:cs + 4], sm[:, 0:4], r2T[:, cs:cs + 4], ALU.mult),
                inc="pv")
            p.barrier()
            if th == 0:
                dbg("h0", h_sb, [128, NHC * 512], BF16)
                dbg("r2T", r2T, [128, 8], F32)
                dbg("ssqT", ssqT, [128, 8], F32)
            sfree("ps_g0", "ps_g1", "ps_u0", "ps_u1", "ps_rms", "ps_trf")

            ps_c2 = [palloc(f"ps_c2_{u}", [128, 512], F32) for u in range(8)]
            wpt_war = [None] * 3
            for oh in range(2):
                mm_final = [None] * 8
                for hc in range(NHC):
                    w = [wpt_war[hc % 3]]
                    if hc == 0:
                        w.append(t_cc2)
                    src = wpT_all_v[hc * 128:(hc + 1) * 128,
                                    oh * 4:(oh + 1) * 4, :]
                    tw = p.dma("sync", wpt[hc % 3][:].rearrange(
                        "p (r o) -> p r o", o=OSH), src, wait=tuple(w),
                        inc=f"w{hc % 3}")
                    for u in range(8):
                        tc4, o5 = u // 2, u % 2
                        if hc == 0:
                            wmm = (tw, h_done[NHC - 1], ps_c2_war[u])
                        elif u == 0:
                            wmm = (tw,)
                        else:
                            wmm = ()
                        mm_final[u] = p.op(
                            "tensor", lambda e, hc=hc, u=u, tc4=tc4, o5=o5:
                            e.matmul(
                                ps_c2[u][:],
                                h_sb[:, hc * 512 + tc4 * 128:
                                     hc * 512 + tc4 * 128 + 128],
                                wpt[hc % 3][:, o5 * 512:(o5 + 1) * 512],
                                start=(hc == 0), stop=(hc == NHC - 1),
                                skip_group_check=True),
                            wait=wmm,
                            inc=("pt" if (hc == NHC - 1 or u == 7) else None))
                    wpt_war[hc % 3] = mm_final[7]
                for u in range(8):
                    tc4, o5 = u // 2, u % 2
                    tcg = th * 4 + tc4
                    col = tcg * DM + oh * 1024 + o5 * 512
                    ps_c2_war[u] = p.op(
                        "vector",
                        lambda e, u=u, tcg=tcg, col=col: e.scalar_tensor_tensor(
                            out=xo2[:, col:col + 512], in0=ps_c2[u][:],
                            scalar=r2T[:, tcg:tcg + 1],
                            in1=xo2[:, col:col + 512],
                            op0=ALU.mult, op1=ALU.add),
                        wait=(mm_final[u], t_r2), inc="pv")
            p.barrier()
            if th == 0:
                dbg("x2a", xo2, [128, NTC * DM], F32)
                if "wpT" in DEBUG_DUMPS:
                    dwp = nc.dram_tensor("dbg_wpT", [1024, OSH], BF16,
                                         kind="ExternalOutput").ap()
                    p.dma("scalar", dwp, wpT_all.ap()[0:1024, :], inc="sq")
            sfree(*[f"ps_c2_{u}" for u in range(8)])

        for tc in range(NTC):
            p.dma("scalar", out_d[tc * 128:(tc + 1) * 128, :],
                  xo2[:, tc * DM:(tc + 1) * DM], inc="sq")
        p.op("sync", lambda e: e.nop(), wait=(("sq", p.cnt["sq"]),))

        p.mark("END")
        p.finalize()

        with nc.Block() as block:
            @block.tensor
            def _(eng):
                p.run(eng, "tensor")

            @block.vector
            def _(eng):
                p.run(eng, "vector")

            @block.scalar
            def _(eng):
                p.run(eng, "scalar")

            @block.gpsimd
            def _(eng):
                p.run(eng, "gpsimd")

            @block.sync
            def _(eng):
                p.run(eng, "sync")

    return nc


def kernel(L0_attn_hidden_states, L0_attn_Wq, L0_attn_Wc, L0_attn_Wup,
           L0_attn_Wo, L0_ffn_w_gate_up, L0_ffn_w_proj):
    x = np.ascontiguousarray(np.asarray(L0_attn_hidden_states, dtype=np.float32))
    Wq = np.ascontiguousarray(np.asarray(L0_attn_Wq, dtype=np.float32))
    Wc = np.ascontiguousarray(np.asarray(L0_attn_Wc, dtype=np.float32))
    Wup = np.ascontiguousarray(np.asarray(L0_attn_Wup, dtype=np.float32))
    Wo = np.ascontiguousarray(np.asarray(L0_attn_Wo, dtype=np.float32))
    wgu = np.ascontiguousarray(np.asarray(L0_ffn_w_gate_up, dtype=np.float32))
    wp = np.ascontiguousarray(np.asarray(L0_ffn_w_proj, dtype=np.float32))

    ones_b = np.ones((128, 128), dtype=bfdt)
    id_b = np.eye(128, dtype=bfdt)
    id_f = np.eye(128, dtype=np.float32)
    ones_f = np.ones((1, 128), dtype=np.float32)

    nc = build_program()
    in_maps = []
    for c in range(NCORES):
        b = c // 4
        t0 = (c % 4) * TOK
        mask = np.zeros((128, 2), dtype=np.float32)
        mask[:, b] = 1.0
        in_maps.append({
            "x_own": np.ascontiguousarray(x[b, t0:t0 + TOK]),
            "Wq": Wq, "Wc": Wc, "Wup": Wup, "Wo": Wo,
            "wgu_sh": np.ascontiguousarray(wgu[c * GSH:(c + 1) * GSH]),
            "wp_sh": np.ascontiguousarray(wp[c * OSH:(c + 1) * OSH]),
            "ones_b": ones_b, "id_b": id_b, "id_f": id_f, "ones_f": ones_f,
            "mask": mask,
        })
    res = run_bass_kernel_spmd(nc, in_maps, core_ids=list(range(NCORES)))
    out = np.empty((B, S, DM), dtype=np.float32)
    for c in range(NCORES):
        b = c // 4
        t0 = (c % 4) * TOK
        out[b, t0:t0 + TOK] = res.results[c]["out"]
    return out
